# version 44
# baseline (speedup 1.0000x reference)
import functools as _ft

import numpy as np
from contextlib import ExitStack

import concourse.bass as bass
import concourse.mybir as mybir
from concourse.bass_utils import run_bass_kernel_spmd

F32 = mybir.dt.float32
BF16 = mybir.dt.bfloat16
I32 = mybir.dt.int32
AF = mybir.ActivationFunctionType
ALU = mybir.AluOpType

P = 128
D = 1024
DC = D // P
T = 2048
H = 3072
HG = 512
NG = H // HG
NHC = HG // P
TTILE = 512
NT = T // TTILE
KT = 128
NKT = T // KT
QW = 256
NQ = T // QW
YROW = T + 1
EPS = 1e-5
DOWN_SCALE = 0.5 / (127.0 * 127.0)
SILU_SCALE = 1.0 / 127.0

PHASES = ("ffn", "attn", "ffn")
N_CORES = 8

_ENGINES = ("sync", "gpsimd", "scalar", "vector", "tensor")
_OWN_SEM = {"tensor": "sPE", "scalar": "sACT", "vector": "sDVE",
            "gpsimd": "sPOOL"}


class _Sem:
    __slots__ = ("h", "n")

    def __init__(self, h):
        self.h = h
        self.n = 0


class _B:

    def __init__(self):
        self.prog = {k: [] for k in _ENGINES}
        self.snap = []
        self.entered = {k: True for k in _ENGINES}
        self.hwm = {k: {} for k in _ENGINES}

    def begin_phase(self):
        self.snap = [(s, s.n) for s in self.all_sems]
        self.entered = {k: False for k in _ENGINES}

    def step(self, eng, waits, emit, inc=None, amt=1, no_snap=False):
        if inc is not None:
            inc.n += amt
        tgt = []
        if not self.entered[eng]:
            self.entered[eng] = True
            if not no_snap:
                tgt += [(s, v) for (s, v) in self.snap if v > 0]
        tgt += [(s, v) for (s, v) in waits if v > 0]
        own = _OWN_SEM.get(eng)
        hw = self.hwm[eng]
        dd = {}
        for s, v in tgt:
            k = id(s)
            if own is not None and s is getattr(self, own, None):
                continue
            if v <= hw.get(k, -1):
                continue
            if k not in dd or dd[k][1] < v:
                dd[k] = (s, v)
        tgt2 = []
        for k, (s, v) in dd.items():
            hw[k] = v
            tgt2.append((s.h, v))
        tgt = tgt2
        pair = (inc.h, amt) if inc is not None else None

        def run(E, tgt=tgt, emit=emit, pair=pair):
            for h, v in tgt:
                E.wait_ge(h, v)
            last = emit(E)
            if pair is not None:
                last.then_inc(pair[0], pair[1])
        self.prog[eng].append(run)


def _x3(b):
    return b.x_sb[:].rearrange("p (j t) -> p j t", j=DC)


class _LN:

    def __init__(self, b, ps_sum, ps_sq):
        self.b = b
        self.ps_sum, self.ps_sq = ps_sum, ps_sq
        self.stats_pe = 0
        self.rows_dve = 0
        self.bev_dve = 0
        self.bev_act = 0
        self.bm_readers = []
        self.br_readers = []
        self._bmev = 0
        self.sub_dve = 0
        self.pool_tail = 0
        self.norm_dve = 0
        self.norm_pool = 0
        self.xadd_pool = 0
        self.tile_act = {}
        self.tile_act_lo = {}
        self.tile_xadd = {}
        self.s1ct = {}
        self.rowsct = {}

    def tile(self, tt, add_to_x, cast_waits=()):
        self.stage1(tt, cast_waits)
        self.stage2a(tt)
        self.stage2a_recip(tt)
        self.stage2b(tt, add_to_x)

    def stage1(self, tt, cast_waits=(), sq_eng="gpsimd"):
        b = self.b
        self.s1ct[tt] = ln_stage1(b, tt, sq_eng=sq_eng,
                                  cast_waits=[(b.sPE, self.stats_pe),
                                          (b.sDVE, self.norm_dve),
                                          (b.sPOOL, self.norm_pool)]
                                  + list(cast_waits))

    def stage2a(self, tt):
        b = self.b
        sPE, sACT, sDVE = b.sPE, b.sACT, b.sDVE
        ps_sum, ps_sq = self.ps_sum, self.ps_sq
        xbf, sq_sb = b.xbf_sb, b.sq_sb
        mrow, vrow, rrow = b.mrow, b.vrow, b.rrow
        s1_waits = list(self.s1ct[tt])

        def stats(E, ps_sum=ps_sum, ps_sq=ps_sq):
            for j in range(DC):
                E.matmul(ps_sum[0:1, :], b.ones_cb[:, 0:1],
                         xbf[:, j * TTILE:(j + 1) * TTILE],
                         start=(j == 0), stop=(j == DC - 1))
            last = None
            for j in range(DC):
                last = E.matmul(ps_sq[0:1, :], b.ones_cb[:, 0:1],
                                sq_sb[:, j * TTILE:(j + 1) * TTILE],
                                start=(j == 0), stop=(j == DC - 1))
            return last
        b.step("tensor", s1_waits + [(sDVE, self.bev_dve),
                                     (sACT, self.bev_act)], stats,
               inc=sPE)
        self.stats_pe = sPE.n

        def rows(E, ps_sum=ps_sum, ps_sq=ps_sq):
            E.tensor_scalar(mrow[0:1, :], ps_sum[0:1, :], 1.0 / D, None, ALU.mult)
            E.tensor_tensor(rrow[0:1, :], mrow[0:1, :], mrow[0:1, :], ALU.mult)
            E.scalar_tensor_tensor(vrow[0:1, :], ps_sq[0:1, :], 1.0 / D,
                                   rrow[0:1, :], ALU.mult, ALU.subtract)
            return E.tensor_copy(b.mrow_bf[0:1, :], mrow[0:1, :])
        b.step("vector", [(sPE, self.stats_pe)], rows, inc=sDVE)
        self.rows_dve = sDVE.n

        b.step("scalar", [(sDVE, self.rows_dve), (sPE, self.stats_pe)],
               lambda E: E.activation(rrow[0:1, :], vrow[0:1, :], AF.Sqrt,
                                      bias=b.eps_col[0:1, 0:1], scale=1.0),
               inc=sACT)
        self.sqrt_ct = sACT.n
        self.rowsct[tt] = (self.rows_dve, None)

    def stage2a_recip(self, tt):
        b = self.b

        def recip(E):
            E.reciprocal(self.b.vrow[0:1, :], self.b.rrow[0:1, :])
            return E.tensor_copy(b.rsr_bf[0:1, :], self.b.vrow[0:1, :])
        b.step("vector", [(b.sACT, self.sqrt_ct)], recip, inc=b.sDVE)
        self.rowsct[tt] = (self.rowsct[tt][0], (b.sDVE, b.sDVE.n))

    def _norm_aps(self, tt):
        b = self.b
        ts0 = tt * TTILE
        H2 = DC // 2
        xbf3 = b.xbf_sb[:].rearrange("p (j t) -> p j t", j=DC)
        sq3 = b.sq_sb[:].rearrange("p (j t) -> p j t", j=DC)
        y3 = b.y_sb[:].rearrange("p (j c) -> p j c", c=YROW)[
            :, :, 1 + ts0:1 + ts0 + TTILE]
        x3t = _x3(b)[:, :, ts0:ts0 + TTILE]
        bm3h = b.bm_sb[:].unsqueeze(1).broadcast_to([P, H2, TTILE])
        br3h = b.br_sb[:].unsqueeze(1).broadcast_to([P, H2, TTILE])
        return xbf3, sq3, y3, x3t, bm3h, br3h, H2

    def stage2b_a(self, tt):
        b = self.b
        sPE, sACT, sDVE, sPOOL = b.sPE, b.sACT, b.sDVE, b.sPOOL
        ps_sum = self.ps_sum
        rows_dve, _ = self.rowsct[tt]
        xbf3, sq3, y3, x3t, bm3h, br3h, H2 = self._norm_aps(tt)

        def bcm(E, ps_sum=ps_sum):
            return E.matmul(ps_sum[:, :], b.ones_rb[0:1, :], b.mrow_bf[0:1, :],
                            start=True, stop=True)
        b.step("tensor", [(sDVE, rows_dve)], bcm, inc=sPE)
        b.step("scalar", [(sPE, sPE.n)] + list(self.bm_readers),
               lambda E, ps_sum=ps_sum: E.activation(
                   b.bm_sb[:], ps_sum[:], AF.Copy), inc=sACT)
        bmev = sACT.n

        def sub_h(E, jh=0):
            sl = slice(jh * H2, (jh + 1) * H2)
            return E.tensor_tensor(sq3[:, sl], xbf3[:, sl], bm3h,
                                   ALU.subtract)
        b.step("vector", [(sPE, self.stats_pe), (sPOOL, self.xadd_pool),
                          (sACT, bmev)],
               _ft.partial(sub_h, jh=0), inc=sDVE)
        b.step("vector", [], _ft.partial(sub_h, jh=1), inc=sDVE)
        self.sub_dve = sDVE.n
        self.bm_readers = [(sDVE, self.sub_dve)]
        self._bmev = bmev

    def stage2b_b(self, tt, add_to_x):
        b = self.b
        sPE, sACT, sDVE, sPOOL = b.sPE, b.sACT, b.sDVE, b.sPOOL
        ps_sq = self.ps_sq
        _, rsr_w = self.rowsct[tt]
        xbf3, sq3, y3, x3t, bm3h, br3h, H2 = self._norm_aps(tt)

        def bcr(E, ps_sq=ps_sq):
            return E.matmul(ps_sq[:, :], b.ones_rb[0:1, :], b.rsr_bf[0:1, :],
                            start=True, stop=True)
        b.step("tensor", [rsr_w], bcr, inc=sPE)
        b.step("scalar", [(sPE, sPE.n)] + list(self.br_readers),
               lambda E, ps_sq=ps_sq: E.activation(
                   b.br_sb[:], ps_sq[:], AF.Copy), inc=sACT)
        self.bev_act = max(sACT.n, self._bmev)

        def mul_h(E, jh=0):
            sl = slice(jh * H2, (jh + 1) * H2)
            return E.tensor_tensor(y3[:, sl], sq3[:, sl], br3h, ALU.mult)
        b.step("vector", [(sACT, sACT.n)], _ft.partial(mul_h, jh=0),
               inc=sDVE)
        lo_ct = sDVE.n
        b.step("vector", [], _ft.partial(mul_h, jh=1), inc=sDVE)
        hi_ct = sDVE.n
        self.norm_dve = hi_ct
        self.br_readers = [(sDVE, hi_ct)]
        self.tile_act[tt] = [(sDVE, hi_ct)]
        self.tile_act_lo[tt] = (sDVE, lo_ct)
        if add_to_x:
            b.step("gpsimd", [(sDVE, hi_ct)],
                   lambda E: E.tensor_tensor(x3t, x3t, y3, ALU.add),
                   inc=sPOOL)
            self.xadd_pool = sPOOL.n
            self.tile_xadd[tt] = sPOOL.n

    def stage2b(self, tt, add_to_x, act_evac=False):
        self.stage2b_a(tt)
        self.stage2b_b(tt, add_to_x)


def ln_stage1(b, tt, cast_waits, sq_eng="gpsimd"):
    sACT, sDVE = b.sACT, b.sDVE
    ts0 = tt * TTILE
    xbf, sq_sb = b.xbf_sb, b.sq_sb
    xbf3 = xbf[:].rearrange("p (j t) -> p j t", j=DC)

    def cast(E, ts0=ts0, xbf3=xbf3):
        return E.activation(xbf3, _x3(b)[:, :, ts0:ts0 + TTILE], AF.Copy)
    b.step("scalar", list(cast_waits), cast, inc=sACT)
    cast_ct = sACT.n

    def sq(E):
        return E.tensor_tensor(sq_sb[:], xbf[:], xbf[:], ALU.mult)
    if sq_eng == "gpsimd":
        b.step("gpsimd", [(sACT, cast_ct)] + list(cast_waits), sq,
               inc=b.sPOOL)
        return [(sACT, cast_ct), (b.sPOOL, b.sPOOL.n)]
    b.step("vector", [(sACT, cast_ct)] + list(cast_waits), sq, inc=sDVE)
    return [(sACT, cast_ct), (sDVE, sDVE.n)]


def emit_ffn_weight_load(b, fb, g, buf, war_pe):
    sW = b.sW
    war = [(b.sPE, war_pe)]
    wg_t, wu_t, wd_t = fb["wg"][buf], fb["wu"][buf], fb["wd"][buf]

    def ld_g(E, g=g, wg_t=wg_t):
        return E.dma_start(
            out=wg_t[:].rearrange("p (j h) -> p j h", j=DC),
            in_=b.wgT_h.rearrange("(j p) h -> p j h", p=P)[:, :, g * HG:(g + 1) * HG])
    b.step("gpsimd", war, ld_g, inc=sW, amt=16)

    def ld_u(E, g=g, wu_t=wu_t):
        return E.dma_start(
            out=wu_t[:].rearrange("p (j h) -> p j h", j=DC),
            in_=b.wuT_h.rearrange("(j p) h -> p j h", p=P)[:, :, g * HG:(g + 1) * HG])
    b.step("gpsimd", [], ld_u, inc=sW, amt=16)

    def ld_d(E, g=g, wd_t=wd_t):
        return E.dma_start(
            out=wd_t[:].rearrange("p (q d) -> p q d", q=NHC),
            in_=b.wdT_h.rearrange("(q p) d -> p q d", p=P)[:, NHC * g:NHC * (g + 1), :])
    b.step("gpsimd", [], ld_d, inc=sW, amt=16)
    b.sw_group[g] = sW.n


def emit_ffn_phase(b, fb, gp, ln, tail_lnb, pre_s1_t0=None, post_s1=None,
                   attn_pre=None):
    sPE, sACT, sDVE, sW = b.sPE, b.sACT, b.sDVE, b.sW
    ps_gl, ps_ul, ps_dn = gp["g"], gp["u"], gp["dn"]
    h_sb, s_tmp = fb["h"], fb["stmp"]
    silu_hist = []
    hm_hist = []
    dn_hist = []
    it = 0
    b.dn_evac_tt = {}

    def block(g, tt, mid=(), pre=()):
        nonlocal it
        for fn in pre:
            fn()
        buf = g % 2
        wg_t, wu_t, wd_t = fb["wg"][buf], fb["wu"][buf], fb["wd"][buf]
        ts0 = tt * TTILE
        for hc in range(NHC):
            ps_g = ps_gl[it % 2]
            ps_u = ps_ul[it % 2]

            def gate(E, hc=hc, ts0=ts0, wg_t=wg_t, ps_g=ps_g, j0=0, j1=DC):
                last = None
                for j in range(j0, j1):
                    last = E.matmul(
                        ps_g[:, :],
                        wg_t[:, j * HG + hc * P: j * HG + hc * P + P],
                        b.y_sb[:, j * YROW + 1 + ts0:
                               j * YROW + 1 + ts0 + TTILE],
                        start=(j == 0), stop=(j == DC - 1))
                return last
            w = [(sW, b.sw_group[g]),
                 ln.tile_act_lo.get(tt, (b.sDVE, 0))]
            if len(silu_hist) >= 2:
                w.append((sACT, silu_hist[-2]))
            b.step("tensor", w, _ft.partial(gate, j0=0, j1=DC // 2))
            w2 = list(ln.tile_act[tt])
            b.step("tensor", w2, _ft.partial(gate, j0=DC // 2, j1=DC),
                   inc=sPE)
            gate_ct = sPE.n

            def up(E, hc=hc, ts0=ts0, wu_t=wu_t, ps_u=ps_u):
                last = None
                for j in range(DC):
                    last = E.matmul(
                        ps_u[:, :],
                        wu_t[:, j * HG + hc * P: j * HG + hc * P + P],
                        b.y_sb[:, j * YROW + 1 + ts0:
                               j * YROW + 1 + ts0 + TTILE],
                        start=(j == 0), stop=(j == DC - 1))
                return last
            w = []
            if len(hm_hist) >= 2:
                w.append((sDVE, hm_hist[-2]))
            b.step("tensor", w, up, inc=sPE)
            gu_ct = sPE.n

            st = s_tmp[it % 2]
            w = [(sPE, gate_ct)]
            if len(hm_hist) >= 2:
                w.append((sDVE, hm_hist[-2]))
            b.step("scalar", w,
                   lambda E, st=st, ps_g=ps_g: E.activation(
                       st[:], ps_g[:], AF.Silu, scale=SILU_SCALE), inc=sACT)
            silu_hist.append(sACT.n)

            hdst = h_sb[:, hc * T + ts0: hc * T + ts0 + TTILE]
            w = [(sACT, silu_hist[-1]), (sPE, gu_ct),
                 (sPE, b.pe_down_end.get((g - 1, tt), 0))]
            b.step("vector", w,
                   lambda E, hdst=hdst, st=st, ps_u=ps_u: E.tensor_tensor(
                       hdst, st[:], ps_u[:], ALU.mult), inc=sDVE)
            hm_hist.append(sDVE.n)
            it += 1
        b.pe_up_end[(g, tt)] = b.sPE.n
        for fn in mid:
            fn()
        hm4 = hm_hist[-NHC:]
        for m in range(DC):
            pd = ps_dn[m % 2]
            if m == 0:
                for q in range(NHC):
                    def dnq(E, q=q, ts0=ts0, wd_t=wd_t, pd=pd):
                        return E.matmul(
                            pd[:, :],
                            wd_t[:, q * D + 0 * P: q * D + 0 * P + P],
                            h_sb[:, q * T + ts0: q * T + ts0 + TTILE],
                            start=(q == 0), stop=(q == NHC - 1),
                            skip_group_check=True)
                    w = [(sDVE, hm4[q])]
                    if q == 0 and len(dn_hist) >= 2:
                        w.append((sDVE, dn_hist[-2]))
                    b.step("tensor", w, dnq,
                           inc=(sPE if q == NHC - 1 else None))
                dn_ct = sPE.n
            else:
                def dn(E, m=m, ts0=ts0, wd_t=wd_t, pd=pd):
                    last = None
                    for q in range(NHC):
                        last = E.matmul(
                            pd[:, :],
                            wd_t[:, q * D + m * P: q * D + m * P + P],
                            h_sb[:, q * T + ts0: q * T + ts0 + TTILE],
                            start=(q == 0), stop=(q == NHC - 1))
                    return last
                w = [(sDVE, hm_hist[-1])]
                if len(dn_hist) >= 2:
                    w.append((sDVE, dn_hist[-2]))
                b.step("tensor", w, dn, inc=sPE)
                dn_ct = sPE.n

            xs = b.x_sb[:, m * T + ts0: m * T + ts0 + TTILE]
            b.step("vector", [(sPE, dn_ct)],
                   lambda E, xs=xs, pd=pd: E.scalar_tensor_tensor(
                       xs, pd[:], DOWN_SCALE, xs, ALU.mult, ALU.add),
                   inc=sDVE)
            dn_hist.append(sDVE.n)
        b.pe_down_end[(g, tt)] = sPE.n
        b.dn_evac_tt[tt] = sDVE.n

    if pre_s1_t0 is not None:
        ln.s1ct[0] = pre_s1_t0
    else:
        ln.stage1(0, cast_waits=[(b.sX, getattr(b, "x_half_ct", 0))])
    ln.bev_dve = max(ln.bev_dve, getattr(b, "_init_mprep_ct", 0))
    cst = getattr(b, "c_ln_state", None) if pre_s1_t0 is not None else None
    if cst is not None:
        ln.stats_pe = cst["stats_pe"]
        ln.rowsct[0] = cst["rowsct0"]
        b.c_ln_state = None
    else:
        ln.stage2a(0)
        ln.stage2a_recip(0)
    ln.stage2b(0, add_to_x=False)
    ln.stage1(1, cast_waits=[(b.sX, getattr(b, "x_full_ct", 0))])
    xw = [(b.sX, getattr(b, "x_full_ct", 0))]
    mids = {
        0: [lambda: ln.stage2a(1)],
        1: [lambda: ln.stage2b_b(1, False)],
        2: [lambda: ln.stage2a(2)],
        3: [lambda: ln.stage2b_b(2, False)],
        4: [lambda: ln.stage2a(3)],
        5: [lambda: ln.stage2b_b(3, False)],
    }
    pres = {
        1: [lambda: ln.stage2b_a(1)],
        3: [lambda: ln.stage2b_a(2)],
        5: [lambda: ln.stage2b_a(3)],
    }
    hooks = {
        0: [lambda: ln.stage2a_recip(1)],
        1: [lambda: ln.stage1(2, cast_waits=xw)],
        2: [lambda: ln.stage2a_recip(2)],
        3: [lambda: ln.stage1(3, cast_waits=xw)],
        4: [lambda: ln.stage2a_recip(3)],
    }
    if post_s1 is not None:
        def _fin():
            post_s1()
            ln.bev_dve = max(ln.bev_dve, b.sDVE.n)
        hooks.setdefault(1, []).append(_fin)
    lnb_acts = {}
    if tail_lnb:
        def s1b(tt):
            return lambda: ln.stage1(tt, cast_waits=[(sDVE, dn_hist[-1])])

        def s2ab(tt):
            return lambda: ln.stage2a(tt)

        def s2rc(tt):
            return lambda: ln.stage2a_recip(tt)

        def s2ba(tt):
            return lambda: ln.stage2b_a(tt)

        def s2bbb(tt):
            def f():
                ln.stage2b_b(tt, add_to_x=True)
                lnb_acts[tt] = ln.tile_act[tt]
            return f

        def s2bb(tt):
            def f():
                ln.stage2b(tt, add_to_x=True)
                lnb_acts[tt] = ln.tile_act[tt]
            return f
        hooks[17] = [s1b(0)]
        mids[18] = [s2ab(0)]
        hooks[18] = [s2rc(0)]
        pres[19] = [s2ba(0)]
        mids[19] = [s2bbb(0)]
        hooks[19] = [s1b(1)]
        mids[20] = [s2ab(1)]
        hooks[20] = [s2rc(1)]
        pres[21] = [s2ba(1)]
        mids[21] = [s2bbb(1)]
        hooks[21] = [s1b(2)]
        mids[22] = [s2ab(2)]
        hooks[22] = [s2rc(2)]
        pres[23] = [s2ba(2)]
        mids[23] = [s2bbb(2)]
        hooks[23] = [lambda: s1b_dve(3)]

        def s1b_dve(tt):
            ln.stage1(tt, cast_waits=[(sDVE, dn_hist[-1])], sq_eng="vector")

        def tail3a():
            ln.stage2a(3)
            bev_live[:] = [(b.sDVE, ln.rows_dve), (sACT, ln.bev_act)]

        def tail3b():
            ln.rowsct[3] = (max(ln.rowsct[3][0], b.sDVE.n), ln.rowsct[3][1])
            ln.stage2a_recip(3)
            ln.stage2b(3, add_to_x=True)
            lnb_acts[3] = ln.tile_act[3]
            bev_live[:] = [(b.sDVE, ln.norm_dve), (sACT, ln.bev_act)]
        if attn_pre is not None:
            ww_t, wr_t = attn_pre
            wwr = b.wwT_h.rearrange("(j p) d -> p j d", p=P)
            ww3 = ww_t[:].rearrange("p (j d) -> p j d", j=DC)

            def ld_ww0():
                def f(E, ww3=ww3, wwr=wwr):
                    return E.dma_start(out=ww3[:, 0:DC // 2, :],
                                       in_=wwr[:, 0:DC // 2, :])
                b.step("gpsimd", [(b.sPE, b.pe_up_end[(NG - 2, NT - 1)])], f,
                       inc=b.sW, amt=16)
                b.aw_cts = [b.sW.n]

            def ld_ww1():
                def f1(E, ww3=ww3, wwr=wwr):
                    return E.dma_start(out=ww3[:, DC // 2:DC, :],
                                       in_=wwr[:, DC // 2:DC, :])
                b.step("gpsimd", [(b.sPE, b.pe_up_end[(NG - 1, NT - 1)])], f1,
                       inc=b.sW, amt=16)
                b.aw_cts.append(b.sW.n)

            def ld_wr():
                def f2(E, wr_t=wr_t):
                    return E.dma_start(
                        out=wr_t[:].rearrange("p (j d) -> p j d", j=DC),
                        in_=b.wrT_h.rearrange("(j p) d -> p j d", p=P))
                b.step("gpsimd", [(b.sPE, b.pe_down_end[(NG - 1, NT - 1)])],
                       f2, inc=b.sW, amt=16)
                b.aw_cts.append(b.sW.n)
            hooks[22] = [ld_ww0] + hooks.get(22, [])
            mids[23] = [ld_ww1] + mids.get(23, [])
            hooks[23] = [ld_wr] + hooks.get(23, [])

    bi = 0
    for pair in range(NG // 2):
        ga, gb_ = 2 * pair, 2 * pair + 1
        for tt in range(NT):
            for g in (ga, gb_):
                block(g, tt, mid=mids.get(bi, ()), pre=pres.get(bi, ()))
                for fn in hooks.get(bi, ()):
                    fn()
                bi += 1
            if tt == NT - 1:
                if ga + 2 < NG:
                    emit_ffn_weight_load(b, fb, ga + 2, ga % 2,
                                         b.pe_down_end[(ga, NT - 1)])
                if gb_ + 2 < NG:
                    emit_ffn_weight_load(b, fb, gb_ + 2, gb_ % 2,
                                         b.pe_down_end[(gb_, NT - 1)])
    if not tail_lnb:
        return None
    bev_live = [(b.sDVE, ln.bev_dve),
                (sACT, ln.bev_act)]
    return {
        "acts": lnb_acts,
        "xadd": ln.tile_xadd,
        "bev": bev_live,
        "tail3": (tail3a, tail3b),
        "silu": silu_hist[-1],
        "hm": hm_hist[-1],
        "dnev": dn_hist[-1],
        "pe": b.sPE.n,
    }


def emit_attn_weight_loads(b, ab, war_up, war_dn):
    sW = b.sW
    ww_t, wr_t = ab["ww"], ab["wr"]
    wwr = b.wwT_h.rearrange("(j p) d -> p j d", p=P)
    ww3 = ww_t[:].rearrange("p (j d) -> p j d", j=DC)

    def ld_ww0(E, ww3=ww3, wwr=wwr):
        return E.dma_start(out=ww3[:, 0:DC // 2, :], in_=wwr[:, 0:DC // 2, :])
    b.step("gpsimd", [(b.sPE, war_up)], ld_ww0, inc=sW, amt=16, no_snap=True)
    ww0 = sW.n

    def ld_ww1(E, ww3=ww3, wwr=wwr):
        return E.dma_start(out=ww3[:, DC // 2:DC, :], in_=wwr[:, DC // 2:DC, :])
    b.step("gpsimd", [], ld_ww1, inc=sW, amt=16)
    ww1 = sW.n

    def ld_wr(E, wr_t=wr_t):
        return E.dma_start(out=wr_t[:].rearrange("p (j d) -> p j d", j=DC),
                           in_=b.wrT_h.rearrange("(j p) d -> p j d", p=P))
    b.step("gpsimd", [(b.sPE, war_dn)], ld_wr, inc=sW, amt=16)
    return (ww0, ww1, sW.n)


def emit_attn_main(b, ab, ap, aw_ct, ytile, aend, pre_cast_c=False,
                   c_pre=None, warm_cb=None):
    sPE, sACT, sDVE = b.sPE, b.sACT, b.sDVE
    v_all, ww_t, wr_t = ab["v"], ab["ww"], ab["wr"]
    reads_bf, smt = ab["reads"], ab["smt"]
    ps_va, ps_vb, ps_s, ps_reads = ap["va"], ap["vb"], ap["s"], ap["reads"]

    vevac_ct = {}
    prev_vevac = [0]
    sm_par = {0: 0, 1: 0}
    intra_par = {0: 0, 1: 0}
    sm_done = {}
    xupd_prev = 0
    pre_s1 = None

    aww0, aww1, awr = aw_ct

    def emit_v_pair(Q):
        for kk in (2 * Q, 2 * Q + 1):
            for half, ps_v in ((0, ps_va), (1, ps_vb)):
                def vmm(E, kk=kk, half=half, ps_v=ps_v, ww_t=ww_t, j0=0,
                        j1=DC):
                    last = None
                    for j in range(j0, j1):
                        last = E.matmul(
                            ps_v[:, :],
                            b.y_sb[:, j * YROW + 1 + kk * KT:
                                   j * YROW + 1 + kk * KT + P],
                            ww_t[:, j * D + half * 512: j * D + half * 512 + 512],
                            start=(j == 0), stop=(j == DC - 1))
                    return last
                w = [(b.sW, aww0), (sACT, prev_vevac[0]),
                     (sACT, aend["silu"])] + list(ytile[kk // 4])
                b.step("tensor", w, _ft.partial(vmm, j0=0, j1=DC // 2),
                       no_snap=True)
                b.step("tensor", [(b.sW, aww1)],
                       _ft.partial(vmm, j0=DC // 2, j1=DC), inc=sPE,
                       no_snap=True)
                v_ct = sPE.n
                if kk == NKT - 1 and half == 1 and c_pre is not None:
                    wg0_t, wu0_t, wd0_t = c_pre

                    def ldg(E, wg0_t=wg0_t):
                        return E.dma_start(
                            out=wg0_t[:].rearrange("p (j h) -> p j h", j=DC),
                            in_=b.wgT_h.rearrange("(j p) h -> p j h", p=P)
                            [:, :, 0:HG])
                    b.step("gpsimd", [(sPE, v_ct)], ldg, inc=b.sW, amt=16)

                    def ldu(E, wu0_t=wu0_t):
                        return E.dma_start(
                            out=wu0_t[:].rearrange("p (j h) -> p j h", j=DC),
                            in_=b.wuT_h.rearrange("(j p) h -> p j h", p=P)
                            [:, :, 0:HG])
                    b.step("gpsimd", [], ldu, inc=b.sW, amt=16)
                    b.c_pre_gu_ct = b.sW.n

                vdst = v_all[:, kk * D + half * 512: kk * D + half * 512 + 512]
                b.step("scalar", [(sPE, v_ct)],
                       lambda E, vdst=vdst, ps_v=ps_v: E.activation(
                           vdst, ps_v[:], AF.Copy), inc=sACT, no_snap=True)
                vevac_ct[(kk, half)] = sACT.n
            prev_vevac[0] = vevac_ct[(kk, 1)]

    def emit_s_sm(Q, kt):
        par = kt % 2
        pss = ps_s[par]

        def smm(E, Q=Q, kt=kt, pss=pss):
            last = None
            for j in range(DC):
                last = E.matmul(
                    pss[:, :],
                    b.y_sb[:, j * YROW + kt * KT: j * YROW + kt * KT + P],
                    b.y_sb[:, j * YROW + 1 + Q * QW:
                           j * YROW + 1 + Q * QW + QW],
                    start=(j == 0), stop=(j == DC - 1))
            return last
        w = [(sDVE, sm_par[par]), (sDVE, aend["hm"])] \
            + list(ytile[max(kt // 4, Q // 2)])
        b.step("tensor", w, smm, inc=sPE, no_snap=True)
        s_ct = sPE.n

        mt_sl = b.mt_sb[:, (2 * Q - kt + 1) * P: (2 * Q - kt + 1) * P + QW]
        sdst = smt[par]
        b.step("vector", [(sPE, s_ct), (sPE, intra_par[par])],
               lambda E, sdst=sdst, pss=pss, mt_sl=mt_sl: E.tensor_tensor(
                   sdst[:], pss[:], mt_sl, ALU.mult), inc=sDVE, no_snap=True)
        sm_par[par] = sDVE.n
        sm_done[(Q, kt)] = sDVE.n

    emit_v_pair(0)
    emit_s_sm(0, 0)
    emit_s_sm(0, 1)
    if warm_cb is not None:
        warm_cb[0]()
    for Q in range(NQ):
        nkt = 2 * Q + 2
        for kt in range(nkt):
            par = kt % 2
            if (Q, kt) not in sm_done:
                emit_s_sm(Q, kt)
            if kt + 2 < nkt and (Q, kt + 2) not in sm_done:
                emit_s_sm(Q, kt + 2)
            sdst = smt[par]

            def intra(E, kt=kt, nkt=nkt, v_all=v_all, ps_reads=ps_reads,
                      sdst=sdst):
                last = None
                for m in range(DC):
                    last = E.matmul(
                        ps_reads[:, m * QW: (m + 1) * QW],
                        v_all[:, kt * D + m * P: kt * D + m * P + P],
                        sdst[:],
                        start=(kt == 0 and m % 2 == 0), stop=(kt == nkt - 1),
                        skip_group_check=True)
                return last
            w = [(sDVE, sm_done[(Q, kt)]), (sACT, vevac_ct[(kt, 1)]),
                 (sDVE, aend["dnev"])] + list(aend["bev"])
            if kt == 0:
                w.append((sDVE, xupd_prev))
            b.step("tensor", w, intra, inc=sPE, no_snap=True)
            intra_par[par] = sPE.n

        b.step("scalar", [(sPE, intra_par[(nkt - 1) % 2]),
                          (sDVE, aend["dnev"])],
               lambda E, reads_bf=reads_bf, ps_reads=ps_reads: E.activation(
                   reads_bf[:], ps_reads[:], AF.Copy), inc=sACT, no_snap=True)
        revac = sACT.n
        if Q + 1 < NQ:
            emit_v_pair(Q + 1)

        def rproj(E, wr_t=wr_t, reads_bf=reads_bf, ps_reads=ps_reads):
            last = None
            for m in range(DC):
                for j2 in range(DC):
                    last = E.matmul(
                        ps_reads[:, m * QW: (m + 1) * QW],
                        wr_t[:, j2 * D + m * P: j2 * D + m * P + P],
                        reads_bf[:, j2 * QW: (j2 + 1) * QW],
                        start=(j2 == 0), stop=(j2 == DC - 1),
                        skip_group_check=True)
            return last
        b.step("tensor", [(sACT, revac), (b.sW, awr)], rproj, inc=sPE,
               no_snap=True)
        rp_ct = sPE.n
        if Q == NQ - 1 and c_pre is not None:
            def ldd(E, wd0_t=c_pre[2]):
                return E.dma_start(
                    out=wd0_t[:].rearrange("p (q d) -> p q d", q=NHC),
                    in_=b.wdT_h.rearrange("(q p) d -> p q d", p=P)
                    [:, 0:NHC, :])
            b.step("gpsimd", [(sPE, rp_ct)], ldd, inc=b.sW, amt=16)
            b.c_pre_d_ct = b.sW.n

        def xupd(E, Q=Q, ps_reads=ps_reads, m0=0, m1=DC):
            last = None
            for m in range(m0, m1):
                xs = b.x_sb[:, m * T + Q * QW: m * T + (Q + 1) * QW]
                last = E.scalar_tensor_tensor(
                    xs, ps_reads[:, m * QW: (m + 1) * QW], b.al_col[:, 0:1],
                    xs, ALU.mult, ALU.add)
            return last
        b.step("vector", [(sPE, rp_ct),
                          (b.sPOOL, aend["xadd"].get(Q // 2, 0))],
               _ft.partial(xupd, m0=0, m1=DC // 2), inc=sDVE, no_snap=True)
        if Q + 1 < NQ:
            emit_s_sm(Q + 1, 0)
        b.step("vector", [], _ft.partial(xupd, m0=DC // 2, m1=DC), inc=sDVE,
               no_snap=True)
        xupd_prev = sDVE.n
        if Q + 1 < NQ:
            emit_s_sm(Q + 1, 1)
        if Q == 0 and warm_cb is not None:
            warm_cb[1]()
        if Q == 1 and pre_cast_c:
            pre_s1 = ln_stage1(b, 0, [(sDVE, xupd_prev), (sPE, aend["pe"])])
        last_revac = revac

    c_state = None
    if pre_cast_c and not getattr(b, "no_c_tail", False):
        psum_row = ps_reads[0:1, 4 * QW:4 * QW + TTILE]
        pssq_row = ps_reads[0:1, 6 * QW:6 * QW + TTILE]

        def cstats(E, psum_row=psum_row, pssq_row=pssq_row):
            for j in range(DC):
                E.matmul(psum_row, b.ones_cb[:, 0:1],
                         b.xbf_sb[:, j * TTILE:(j + 1) * TTILE],
                         start=(j == 0), stop=(j == DC - 1))
            last = None
            for j in range(DC):
                last = E.matmul(pssq_row, b.ones_cb[:, 0:1],
                                b.sq_sb[:, j * TTILE:(j + 1) * TTILE],
                                start=(j == 0), stop=(j == DC - 1))
            return last
        b.step("tensor", list(pre_s1) + [(sDVE, xupd_prev),
                                         (sACT, last_revac)],
               cstats, inc=sPE, no_snap=True)
        c_stats_pe = sPE.n

        def crows(E, psum_row=psum_row, pssq_row=pssq_row):
            E.tensor_scalar(b.mrow[0:1, :], psum_row, 1.0 / D, None, ALU.mult)
            E.tensor_tensor(b.rrow[0:1, :], b.mrow[0:1, :], b.mrow[0:1, :],
                            ALU.mult)
            E.scalar_tensor_tensor(b.vrow[0:1, :], pssq_row, 1.0 / D,
                                   b.rrow[0:1, :], ALU.mult, ALU.subtract)
            return E.tensor_copy(b.mrow_bf[0:1, :], b.mrow[0:1, :])
        b.step("vector", [(sPE, c_stats_pe)], crows, inc=sDVE, no_snap=True)
        c_rows_dve = sDVE.n

        b.step("scalar", [(sDVE, c_rows_dve)],
               lambda E: E.activation(b.rrow[0:1, :], b.vrow[0:1, :], AF.Sqrt,
                                      bias=b.eps_col[0:1, 0:1], scale=1.0),
               inc=sACT, no_snap=True)
        c_sqrt = sACT.n

        def crecip(E):
            E.reciprocal(b.vrow[0:1, :], b.rrow[0:1, :])
            return E.tensor_copy(b.rsr_bf[0:1, :], b.vrow[0:1, :])
        b.step("vector", [(sACT, c_sqrt)], crecip, inc=sDVE, no_snap=True)
        c_state = {"stats_pe": c_stats_pe,
                   "rowsct0": (c_rows_dve, (sDVE, sDVE.n))}
    b.c_ln_state = c_state
    return pre_s1 if pre_cast_c else None


def emit_init(b, ps_init):
    sX, sPE, sACT, sDVE = b.sX, b.sPE, b.sACT, b.sDVE

    xr = b.xT_h.rearrange("(j p) t -> p j t", p=P)
    b.step("sync", [], lambda E: E.dma_start(out=b.scal_sb[0:1, 0:1], in_=b.dec_h),
           inc=sX, amt=16)
    b.step("sync", [], lambda E: E.dma_start(out=b.scal_sb[0:1, 1:2], in_=b.la_h),
           inc=sX, amt=16)
    scal_dma_ct = sX.n

    b.step("sync", [], lambda E: E.dma_start(out=b.g_sb[:], in_=b.G_h), inc=sX,
           amt=16)
    b.step("sync", [], lambda E: E.dma_start(out=b.caus_sb[:], in_=b.caus_h),
           inc=sX, amt=16)
    g_ct = sX.n

    for jj in range(4):
        def ld_x0j(E, jj=jj, xr=xr):
            return E.dma_start(out=_x3(b)[:, 2 * jj:2 * jj + 2, 0:TTILE],
                               in_=xr[:, 2 * jj:2 * jj + 2, 0:TTILE])
        b.step("sync", [], ld_x0j, inc=sX, amt=16)
    b.x_half_ct = sX.n
    x_ct = sX.n

    def ld_x1(E, xr=xr):
        return E.dma_start(out=_x3(b)[:, :, TTILE:T], in_=xr[:, :, TTILE:T])
    b.step("sync", [], ld_x1, inc=sX, amt=16)
    b.x_full_ct = sX.n

    def consts(E):
        E.memset(b.ones_f[:], 1.0)
        E.memset(b.ones_rb[:], 1.0)
        E.memset(b.ones_cb[:], 1.0)
        E.memset(b.eps_col[:], EPS)
        E.memset(b.mt_sb[:, 0:P], 0.0)
        last = None
        for j in range(DC):
            last = E.memset(b.y_sb[:, j * YROW: j * YROW + 1], 0.0)
        return last
    b.step("vector", [], consts, inc=sDVE)
    const_ct = sDVE.n

    def scal(E):
        E.activation(b.scal_sb[0:1, 2:3], b.scal_sb[0:1, 0:1], AF.Sigmoid)
        E.activation(b.scal_sb[0:1, 3:4], b.scal_sb[0:1, 2:3], AF.Ln)
        return E.activation(b.scal_sb[0:1, 4:5], b.scal_sb[0:1, 1:2], AF.Exp)
    b.step("scalar", [(sX, scal_dma_ct)], scal, inc=sACT)
    scal_ct = sACT.n

    def bc(E, ps_init=ps_init):
        E.matmul(ps_init[:, 0:1], b.ones_f[0:1, :], b.scal_sb[0:1, 3:4],
                 start=True, stop=True)
        return E.matmul(ps_init[:, 1:2], b.ones_f[0:1, :], b.scal_sb[0:1, 4:5],
                        start=False, stop=True)
    b.step("tensor", [(sACT, scal_ct), (sDVE, const_ct)], bc, inc=sPE)
    bc_ct = sPE.n

    def mprep_dve(E, ps_init=ps_init):
        E.tensor_copy(b.lg_col[:], ps_init[:, 0:1])
        E.tensor_copy(b.al_col[:], ps_init[:, 1:2])
        for i in range(1, 17):
            E.tensor_scalar(b.bias_cols[:, i:i + 1], b.lg_col[:],
                            float(128 * (i - 1) - 1), None, ALU.mult)
        E.tensor_scalar(b.ytmp[0][:, 0:P], b.g_sb[:], b.lg_col[:, 0:1],
                        b.lg_col[:, 0:1], ALU.mult, ALU.subtract)
        return E.tensor_scalar(b.ytmp[1][:, 0:P], b.ytmp[0][:, 0:P], 0.0, None,
                               ALU.min)
    b.step("vector", [(sPE, bc_ct), (sX, g_ct)], mprep_dve, inc=sDVE)
    b._init_mprep_ct = sDVE.n


def emit_init_mprep(b, ps_init):
    sPE, sACT, sDVE = b.sPE, b.sACT, b.sDVE
    mprep_ct = b._init_mprep_ct

    def mprep_act(E):
        E.activation(b.ytmp[0][:, 0:P], b.ytmp[1][:, 0:P], AF.Exp)
        last = None
        for i in range(2, 17):
            last = E.activation(b.mt_sb[:, i * P:(i + 1) * P], b.g_sb[:], AF.Exp,
                                bias=b.bias_cols[:, i:i + 1], scale=b.lg_col[:, 0:1])
        return last
    b.step("scalar", [(sDVE, mprep_ct)], mprep_act, inc=sACT)
    mact_ct = sACT.n

    b.step("vector", [(sACT, mact_ct)],
           lambda E: E.tensor_tensor(b.mt_sb[:, P:2 * P], b.ytmp[0][:, 0:P],
                                     b.caus_sb[:], ALU.mult), inc=sDVE)


def build(phases=None):
    phases = PHASES if phases is None else phases
    nc = bass.Bass("TRN2", target_bir_lowering=False, debug=False)
    b = _B()

    b.xT_h = nc.dram_tensor("xT", [D, T], F32, kind="ExternalInput").ap()
    b.wwT_h = nc.dram_tensor("wwT", [D, D], F32, kind="ExternalInput").ap()
    b.wrT_h = nc.dram_tensor("wrT", [D, D], F32, kind="ExternalInput").ap()
    b.wgT_h = nc.dram_tensor("wgT", [D, H], I32, kind="ExternalInput").ap()
    b.wuT_h = nc.dram_tensor("wuT", [D, H], I32, kind="ExternalInput").ap()
    b.wdT_h = nc.dram_tensor("wdT", [H, D], I32, kind="ExternalInput").ap()
    b.dec_h = nc.dram_tensor("dec", [1, 1], F32, kind="ExternalInput").ap()
    b.la_h = nc.dram_tensor("la", [1, 1], F32, kind="ExternalInput").ap()
    b.G_h = nc.dram_tensor("G", [P, P], F32, kind="ExternalInput").ap()
    b.caus_h = nc.dram_tensor("causT", [P, P], F32, kind="ExternalInput").ap()
    b.yT_h = nc.dram_tensor("yT", [D, T], F32, kind="ExternalOutput").ap()

    with ExitStack() as ctx:
        e = ctx.enter_context
        b.x_sb = e(nc.sbuf_tensor("x_sb", [P, DC * T], F32))
        b.y_sb = e(nc.sbuf_tensor("y_sb", [P, DC * YROW], BF16))
        b.xbf_sb = e(nc.sbuf_tensor("xbf_sb", [P, DC * TTILE], BF16))
        b.sq_sb = e(nc.sbuf_tensor("sq_sb", [P, DC * TTILE], BF16))
        b.mt_sb = e(nc.sbuf_tensor("mt_sb", [P, 17 * P], F32))
        b.bm_sb = e(nc.sbuf_tensor("bm_sb", [P, TTILE], BF16))
        b.br_sb = e(nc.sbuf_tensor("br_sb", [P, TTILE], BF16))
        b.mrow_bf = e(nc.sbuf_tensor("mrow_bf", [1, TTILE], BF16))
        b.rsr_bf = e(nc.sbuf_tensor("rsr_bf", [1, TTILE], BF16))
        b.ones_rb = e(nc.sbuf_tensor("ones_rb", [1, P], BF16))
        b.ytmp = [e(nc.sbuf_tensor(f"ytmp{i}", [P, TTILE], F32)) for i in range(2)]
        b.g_sb = e(nc.sbuf_tensor("g_sb", [P, P], F32))
        b.caus_sb = e(nc.sbuf_tensor("caus_sb", [P, P], F32))
        b.ones_f = e(nc.sbuf_tensor("ones_f", [1, P], F32))
        b.ones_cb = e(nc.sbuf_tensor("ones_cb", [P, 1], BF16))
        b.lg_col = e(nc.sbuf_tensor("lg_col", [P, 1], F32))
        b.al_col = e(nc.sbuf_tensor("al_col", [P, 1], F32))
        b.eps_col = e(nc.sbuf_tensor("eps_col", [P, 1], F32))
        b.bias_cols = e(nc.sbuf_tensor("bias_cols", [P, 17], F32))
        b.scal_sb = e(nc.sbuf_tensor("scal_sb", [1, 8], F32))
        b.mrow = e(nc.sbuf_tensor("mrow", [1, TTILE], F32))
        b.vrow = e(nc.sbuf_tensor("vrow", [1, TTILE], F32))
        b.rrow = e(nc.sbuf_tensor("rrow", [1, TTILE], F32))

        b.sX = _Sem(e(nc.semaphore("sX")))
        b.sW = _Sem(e(nc.semaphore("sW")))
        b.sO = _Sem(e(nc.semaphore("sO")))
        b.sPE = _Sem(e(nc.semaphore("sPE")))
        b.sACT = _Sem(e(nc.semaphore("sACT")))
        b.sDVE = _Sem(e(nc.semaphore("sDVE")))
        b.sPOOL = _Sem(e(nc.semaphore("sPOOL")))
        b.all_sems = [b.sX, b.sW, b.sO, b.sPE, b.sACT, b.sDVE, b.sPOOL]
        b.sw_group = {}
        b.pe_down_end = {}
        b.pe_up_end = {}

        b.begin_phase()
        with nc.psum_tensor("ps_init", [P, 4], F32) as ps_init:
            emit_init(b, ps_init)
        xbf3 = b.xbf_sb[:].rearrange("p (j t) -> p j t", j=DC)

        def cast0(E, xbf3=xbf3):
            return E.activation(xbf3, _x3(b)[:, :, 0:TTILE], AF.Copy)
        b.step("scalar", [(b.sX, b.x_half_ct)], cast0, inc=b.sACT)
        _xbf_act = b.sACT.n

        def sq0(E):
            return E.tensor_tensor(b.sq_sb[:], b.xbf_sb[:], b.xbf_sb[:],
                                   ALU.mult)
        b.step("vector", [(b.sACT, _xbf_act)], sq0, inc=b.sDVE)
        pre_s1_first = [(b.sACT, _xbf_act), (b.sDVE, b.sDVE.n)]
        init_state = {"ps_init": ps_init, "done": False}

        def finish_init():
            if not init_state["done"]:
                emit_init_mprep(b, init_state["ps_init"])
                init_state["done"] = True

        first_phase = True
        lnb_carry = None
        pre_s1_carry = pre_s1_first
        pi = 0
        for idx, ph in enumerate(phases):
            pi += 1
            nxt = phases[idx + 1] if idx + 1 < len(phases) else None
            if ph == "ffn":
                cp = getattr(b, "c_pre", None) if not first_phase else None
                with ExitStack() as fs:
                    f = fs.enter_context
                    if cp is not None:
                        fb = {
                            "h": f(nc.sbuf_tensor(f"h_sb{pi}", [P, NHC * T],
                                                  BF16)),
                            "wg": [cp[0],
                                   f(nc.sbuf_tensor(f"wg{pi}_1", [P, DC * HG],
                                                    BF16))],
                            "wu": [cp[1],
                                   f(nc.sbuf_tensor(f"wu{pi}_1", [P, DC * HG],
                                                    BF16))],
                        }
                        f(nc.sbuf_tensor(f"hold{pi}", [P, 3 * 4096], BF16))
                        fb["wd"] = [cp[2],
                                    f(nc.sbuf_tensor(f"wd{pi}_1", [P, NHC * D],
                                                     BF16))]
                        fb["stmp"] = [f(nc.sbuf_tensor(f"st{pi}_{i}",
                                                       [P, TTILE], BF16))
                                      for i in range(2)]
                        b.c_pre = None
                    else:
                        fb = {
                            "h": f(nc.sbuf_tensor(f"h_sb{pi}", [P, NHC * T],
                                                  BF16)),
                            "wg": [f(nc.sbuf_tensor(f"wg{pi}_{i}",
                                                    [P, DC * HG], BF16))
                                   for i in range(2)],
                            "wu": [f(nc.sbuf_tensor(f"wu{pi}_{i}",
                                                    [P, DC * HG], BF16))
                                   for i in range(2)],
                            "wd": [f(nc.sbuf_tensor(f"wd{pi}_{i}",
                                                    [P, NHC * D], BF16))
                                   for i in range(2)],
                            "stmp": [f(nc.sbuf_tensor(f"st{pi}_{i}",
                                                      [P, TTILE], BF16))
                                     for i in range(2)],
                        }
                    if not first_phase:
                        b.begin_phase()
                    b.pe_down_end = {}
                    b.pe_up_end = {}
                    b.sw_group = {}
                    if cp is not None:
                        b.sw_group[0] = max(b.c_pre_gu_ct, b.c_pre_d_ct)
                    else:
                        emit_ffn_weight_load(b, fb, 0, 0, 0)
                    emit_ffn_weight_load(b, fb, 1, 1, 0)
                    with ExitStack() as ls:
                        l = ls.enter_context
                        gp = {
                            "g": [l(nc.psum_tensor(f"ps_g{pi}_{i}",
                                                   [P, TTILE], F32))
                                  for i in range(2)],
                            "u": [l(nc.psum_tensor(f"ps_u{pi}_{i}",
                                                   [P, TTILE], F32))
                                  for i in range(2)],
                            "dn": [l(nc.psum_tensor(f"ps_dn{pi}_{i}",
                                                    [P, TTILE], F32))
                                   for i in range(2)],
                        }
                        ps_sum = l(nc.psum_tensor(f"ps_sum{pi}", [P, TTILE], F32))
                        ps_sq = l(nc.psum_tensor(f"ps_sq{pi}", [P, TTILE], F32))
                        ln = _LN(b, ps_sum, ps_sq)
                        attn_pre = None
                        if nxt == "attn":
                            base = nc.lookup_mloc(fb["h"]).addr
                            ww_pre = nc.alloc_sbuf_tensor_at(
                                "ww_pre", [P, DC * D], BF16,
                                offset=base + 4 * 8192)
                            wr_pre = nc.alloc_sbuf_tensor_at(
                                "wr_pre", [P, DC * D], BF16,
                                offset=base + 6 * 8192)
                            attn_pre = (ww_pre, wr_pre)
                            b.attn_pre = attn_pre
                        lnb_carry = emit_ffn_phase(
                            b, fb, gp, ln, nxt == "attn",
                            pre_s1_t0=pre_s1_carry,
                            post_s1=(finish_init if not init_state["done"]
                                     else None), attn_pre=attn_pre)
                        pre_s1_carry = None
            elif ph == "attn":
                with ExitStack() as as_:
                    a = as_.enter_context
                    pre = getattr(b, "attn_pre", None) if lnb_carry is not None \
                        else None
                    ab = {"v": a(nc.sbuf_tensor("v_all", [P, NKT * D], BF16))}
                    if pre is not None:
                        a(nc.sbuf_tensor("wwr_hold", [P, DC * D * 2], BF16))
                        ab["ww"], ab["wr"] = pre
                    else:
                        ab["ww"] = a(nc.sbuf_tensor("ww_bf", [P, DC * D], BF16))
                        ab["wr"] = a(nc.sbuf_tensor("wr_bf", [P, DC * D], BF16))
                    ab["reads"] = a(nc.sbuf_tensor("reads_bf", [P, DC * QW],
                                                   BF16))
                    ab["smt"] = [a(nc.sbuf_tensor(f"smt{i}", [P, QW], BF16))
                                 for i in range(2)]
                    finish_init()
                    if pre is not None:
                        aw_ct = tuple(b.aw_cts)
                    else:
                        war_up = b.pe_up_end.get((NG - 1, NT - 1), 0)
                        war_dn = b.pe_down_end.get((NG - 1, NT - 1), 0)
                        aw_ct = emit_attn_weight_loads(b, ab, war_up, war_dn)
                    if lnb_carry is not None:
                        ytile = lnb_carry["acts"]
                        aend = {k: lnb_carry[k] for k in
                                ("xadd", "bev", "silu", "hm", "dnev", "pe")}
                    else:
                        with ExitStack() as ls:
                            l = ls.enter_context
                            ps_sum = l(nc.psum_tensor("ps_sumB", [P, TTILE], F32))
                            ps_sq = l(nc.psum_tensor("ps_sqB", [P, TTILE], F32))
                            b.begin_phase()
                            lnx = _LN(b, ps_sum, ps_sq)
                            for tt in range(NT):
                                lnx.tile(tt, add_to_x=True)
                            ytile = dict(lnx.tile_act)
                            aend = {"xadd": dict(lnx.tile_xadd),
                                    "bev": [(b.sDVE, lnx.bev_dve),
                                            (b.sACT, lnx.bev_act)],
                                    "silu": b.sACT.n,
                                    "hm": b.sDVE.n, "dnev": b.sDVE.n,
                                    "pe": b.sPE.n}
                    with ExitStack() as ms:
                        m_ = ms.enter_context
                        ap = {
                            "va": m_(nc.psum_tensor("ps_va", [P, 512], F32)),
                            "vb": m_(nc.psum_tensor("ps_vb", [P, 512], F32)),
                            "s": [m_(nc.psum_tensor(f"ps_s{i}", [P, QW], F32))
                                  for i in range(2)],
                            "reads": m_(nc.psum_tensor("ps_reads", [P, DC * QW],
                                                       F32)),
                        }
                        c_pre = None
                        if nxt == "ffn":
                            base_b = nc.lookup_mloc(ab["v"]).addr
                            c_pre = (
                                nc.alloc_sbuf_tensor_at(
                                    "c_wg0", [P, DC * HG], BF16,
                                    offset=base_b + 4 * 8192),
                                nc.alloc_sbuf_tensor_at(
                                    "c_wu0", [P, DC * HG], BF16,
                                    offset=base_b + 5 * 8192),
                                nc.alloc_sbuf_tensor_at(
                                    "c_wd0", [P, NHC * D], BF16,
                                    offset=base_b + 6 * 8192),
                            )
                            b.c_pre = c_pre
                        pre_s1_carry = emit_attn_main(
                            b, ab, ap, aw_ct, ytile, aend,
                            pre_cast_c=(nxt == "ffn"), c_pre=c_pre,
                            warm_cb=(lnb_carry or {}).get("tail3"))
                lnb_carry = None
            else:
                raise ValueError(ph)
            first_phase = False

        yr = b.yT_h.rearrange("(j p) t -> p j t", p=P)
        if phases[-1] == "ffn" and len(getattr(b, "dn_evac_tt", {})) == NT:
            for tt in range(NT):
                ts0 = tt * TTILE

                def out_t(E, ts0=ts0):
                    return E.dma_start(out=yr[:, :, ts0:ts0 + TTILE],
                                       in_=_x3(b)[:, :, ts0:ts0 + TTILE])
                b.step("sync", [(b.sDVE, b.dn_evac_tt[tt])], out_t,
                       inc=b.sO, amt=16)
            b.step("sync", [(b.sO, b.sO.n)], lambda E: E.nop())
        else:
            b.begin_phase()

            def out(E):
                return E.dma_start(out=yr, in_=_x3(b))
            b.step("sync", [], out, inc=b.sO, amt=16)
            b.step("sync", [(b.sO, b.sO.n)], lambda E: E.nop())

        block = e(nc.Block())
        for name in _ENGINES:
            fns = b.prog[name]
            getattr(block, name)(lambda E, fns=fns: [f(E) for f in fns])
    return nc


def kernel(L0_attn_out, L0_attn_W_write, L0_attn_W_read, L0_attn_decay,
           L0_attn_log_alpha, L0_ffn_w_gate, L0_ffn_w_up, L0_ffn_w_down):
    x = np.ascontiguousarray(np.asarray(L0_attn_out, dtype=np.float32))
    wwT = np.ascontiguousarray(np.asarray(L0_attn_W_write, dtype=np.float32).T)
    wrT = np.ascontiguousarray(np.asarray(L0_attn_W_read, dtype=np.float32).T)
    wgT = np.ascontiguousarray(np.asarray(L0_ffn_w_gate, dtype=np.int32).T)
    wuT = np.ascontiguousarray(np.asarray(L0_ffn_w_up, dtype=np.int32).T)
    wdT = np.ascontiguousarray(np.asarray(L0_ffn_w_down, dtype=np.int32).T)
    dec = np.asarray(L0_attn_decay, dtype=np.float32).reshape(1, 1)
    la = np.asarray(L0_attn_log_alpha, dtype=np.float32).reshape(1, 1)
    idx = np.arange(P, dtype=np.float32)
    G = (idx[None, :] - idx[:, None]).astype(np.float32)
    caus = (idx[None, :] > idx[:, None]).astype(np.float32)

    nc = build()
    in_maps = []
    for bb in range(N_CORES):
        in_maps.append({
            "xT": np.ascontiguousarray(x[bb].T), "wwT": wwT, "wrT": wrT,
            "wgT": wgT, "wuT": wuT, "wdT": wdT, "dec": dec, "la": la,
            "G": G, "causT": caus,
        })
    res = run_bass_kernel_spmd(nc, in_maps, core_ids=list(range(N_CORES)))
    out = np.stack([np.asarray(res.results[bb]["yT"]).T for bb in range(N_CORES)],
                   axis=0)
    return np.ascontiguousarray(out, dtype=np.float32)


# version 45
# speedup vs baseline: 1.0165x; 1.0165x over previous
import functools as _ft

import numpy as np
from contextlib import ExitStack

import concourse.bass as bass
import concourse.mybir as mybir
from concourse.bass_utils import run_bass_kernel_spmd

F32 = mybir.dt.float32
BF16 = mybir.dt.bfloat16
I32 = mybir.dt.int32
AF = mybir.ActivationFunctionType
ALU = mybir.AluOpType

P = 128
D = 1024
DC = D // P
T = 2048
H = 3072
HG = 512
NG = H // HG
NHC = HG // P
TTILE = 512
NT = T // TTILE
KT = 128
NKT = T // KT
QW = 256
NQ = T // QW
YROW = T + 1
EPS = 1e-5
DOWN_SCALE = 0.5 / (127.0 * 127.0)
SILU_SCALE = 1.0 / 127.0

PHASES = ("ffn", "attn", "ffn")
N_CORES = 8

_ENGINES = ("sync", "gpsimd", "scalar", "vector", "tensor")
_OWN_SEM = {"tensor": "sPE", "scalar": "sACT", "vector": "sDVE",
            "gpsimd": "sPOOL"}


class _Sem:
    __slots__ = ("h", "n")

    def __init__(self, h):
        self.h = h
        self.n = 0


class _B:

    def __init__(self):
        self.prog = {k: [] for k in _ENGINES}
        self.snap = []
        self.entered = {k: True for k in _ENGINES}
        self.hwm = {k: {} for k in _ENGINES}

    def begin_phase(self):
        self.snap = [(s, s.n) for s in self.all_sems]
        self.entered = {k: False for k in _ENGINES}

    def step(self, eng, waits, emit, inc=None, amt=1, no_snap=False):
        if inc is not None:
            inc.n += amt
        tgt = []
        if not self.entered[eng]:
            self.entered[eng] = True
            if not no_snap:
                tgt += [(s, v) for (s, v) in self.snap if v > 0]
        tgt += [(s, v) for (s, v) in waits if v > 0]
        own = _OWN_SEM.get(eng)
        hw = self.hwm[eng]
        dd = {}
        for s, v in tgt:
            k = id(s)
            if own is not None and s is getattr(self, own, None):
                continue
            if v <= hw.get(k, -1):
                continue
            if k not in dd or dd[k][1] < v:
                dd[k] = (s, v)
        tgt2 = []
        for k, (s, v) in dd.items():
            hw[k] = v
            tgt2.append((s.h, v))
        tgt = tgt2
        pair = (inc.h, amt) if inc is not None else None

        def run(E, tgt=tgt, emit=emit, pair=pair):
            for h, v in tgt:
                E.wait_ge(h, v)
            last = emit(E)
            if pair is not None:
                last.then_inc(pair[0], pair[1])
        self.prog[eng].append(run)


def _x3(b):
    return b.x_sb[:].rearrange("p (j t) -> p j t", j=DC)


class _LN:

    def __init__(self, b, ps_sum, ps_sq):
        self.b = b
        self.ps_sum, self.ps_sq = ps_sum, ps_sq
        self.stats_pe = 0
        self.rows_dve = 0
        self.bev_dve = 0
        self.bev_act = 0
        self.bm_readers = []
        self.br_readers = []
        self._bmev = 0
        self.sub_dve = 0
        self.pool_tail = 0
        self.norm_dve = 0
        self.norm_pool = 0
        self.xadd_pool = 0
        self.tile_act = {}
        self.tile_act_lo = {}
        self.tile_xadd = {}
        self.s1ct = {}
        self.rowsct = {}

    def tile(self, tt, add_to_x, cast_waits=()):
        self.stage1(tt, cast_waits)
        self.stage2a(tt)
        self.stage2a_recip(tt)
        self.stage2b(tt, add_to_x)

    def stage1(self, tt, cast_waits=(), sq_eng="gpsimd"):
        b = self.b
        self.s1ct[tt] = ln_stage1(b, tt, sq_eng=sq_eng,
                                  cast_waits=[(b.sPE, self.stats_pe),
                                          (b.sDVE, self.norm_dve),
                                          (b.sPOOL, self.norm_pool)]
                                  + list(cast_waits))

    def stage2a(self, tt):
        b = self.b
        sPE, sACT, sDVE = b.sPE, b.sACT, b.sDVE
        ps_sum, ps_sq = self.ps_sum, self.ps_sq
        xbf, sq_sb = b.xbf_sb, b.sq_sb
        mrow, vrow, rrow = b.mrow, b.vrow, b.rrow
        s1_waits = list(self.s1ct[tt])

        def stats(E, ps_sum=ps_sum, ps_sq=ps_sq):
            for j in range(DC):
                E.matmul(ps_sum[0:1, :], b.ones_cb[:, 0:1],
                         xbf[:, j * TTILE:(j + 1) * TTILE],
                         start=(j == 0), stop=(j == DC - 1))
            last = None
            for j in range(DC):
                last = E.matmul(ps_sq[0:1, :], b.ones_cb[:, 0:1],
                                sq_sb[:, j * TTILE:(j + 1) * TTILE],
                                start=(j == 0), stop=(j == DC - 1))
            return last
        b.step("tensor", s1_waits + [(sDVE, self.bev_dve),
                                     (sACT, self.bev_act)], stats,
               inc=sPE)
        self.stats_pe = sPE.n

        def rows(E, ps_sum=ps_sum, ps_sq=ps_sq):
            E.tensor_scalar(mrow[0:1, :], ps_sum[0:1, :], 1.0 / D, None, ALU.mult)
            E.tensor_tensor(rrow[0:1, :], mrow[0:1, :], mrow[0:1, :], ALU.mult)
            E.scalar_tensor_tensor(vrow[0:1, :], ps_sq[0:1, :], 1.0 / D,
                                   rrow[0:1, :], ALU.mult, ALU.subtract)
            return E.tensor_copy(b.mrow_bf[0:1, :], mrow[0:1, :])
        b.step("vector", [(sPE, self.stats_pe)], rows, inc=sDVE)
        self.rows_dve = sDVE.n

        b.step("scalar", [(sDVE, self.rows_dve), (sPE, self.stats_pe)],
               lambda E: E.activation(rrow[0:1, :], vrow[0:1, :], AF.Sqrt,
                                      bias=b.eps_col[0:1, 0:1], scale=1.0),
               inc=sACT)
        self.sqrt_ct = sACT.n
        self.rowsct[tt] = (self.rows_dve, None)

    def stage2a_recip(self, tt):
        b = self.b

        def recip(E):
            E.reciprocal(self.b.vrow[0:1, :], self.b.rrow[0:1, :])
            return E.tensor_copy(b.rsr_bf[0:1, :], self.b.vrow[0:1, :])
        b.step("vector", [(b.sACT, self.sqrt_ct)], recip, inc=b.sDVE)
        self.rowsct[tt] = (self.rowsct[tt][0], (b.sDVE, b.sDVE.n))

    def _norm_aps(self, tt):
        b = self.b
        ts0 = tt * TTILE
        H2 = DC // 2
        xbf3 = b.xbf_sb[:].rearrange("p (j t) -> p j t", j=DC)
        sq3 = b.sq_sb[:].rearrange("p (j t) -> p j t", j=DC)
        y3 = b.y_sb[:].rearrange("p (j c) -> p j c", c=YROW)[
            :, :, 1 + ts0:1 + ts0 + TTILE]
        x3t = _x3(b)[:, :, ts0:ts0 + TTILE]
        bm3h = b.bm_sb[:].unsqueeze(1).broadcast_to([P, H2, TTILE])
        br3h = b.br_sb[:].unsqueeze(1).broadcast_to([P, H2, TTILE])
        return xbf3, sq3, y3, x3t, bm3h, br3h, H2

    def stage2b_a(self, tt):
        b = self.b
        sPE, sACT, sDVE, sPOOL = b.sPE, b.sACT, b.sDVE, b.sPOOL
        ps_sum = self.ps_sum
        rows_dve, _ = self.rowsct[tt]
        xbf3, sq3, y3, x3t, bm3h, br3h, H2 = self._norm_aps(tt)

        def bcm(E, ps_sum=ps_sum):
            return E.matmul(ps_sum[:, :], b.ones_rb[0:1, :], b.mrow_bf[0:1, :],
                            start=True, stop=True)
        b.step("tensor", [(sDVE, rows_dve)], bcm, inc=sPE)
        b.step("scalar", [(sPE, sPE.n)] + list(self.bm_readers),
               lambda E, ps_sum=ps_sum: E.activation(
                   b.bm_sb[:], ps_sum[:], AF.Copy), inc=sACT)
        bmev = sACT.n

        def sub_h(E, jh=0):
            sl = slice(jh * H2, (jh + 1) * H2)
            return E.tensor_tensor(sq3[:, sl], xbf3[:, sl], bm3h,
                                   ALU.subtract)
        b.step("vector", [(sPE, self.stats_pe), (sPOOL, self.xadd_pool),
                          (sACT, bmev)],
               _ft.partial(sub_h, jh=0), inc=sDVE)
        b.step("vector", [], _ft.partial(sub_h, jh=1), inc=sDVE)
        self.sub_dve = sDVE.n
        self.bm_readers = [(sDVE, self.sub_dve)]
        self._bmev = bmev

    def stage2b_b(self, tt, add_to_x):
        b = self.b
        sPE, sACT, sDVE, sPOOL = b.sPE, b.sACT, b.sDVE, b.sPOOL
        ps_sq = self.ps_sq
        _, rsr_w = self.rowsct[tt]
        xbf3, sq3, y3, x3t, bm3h, br3h, H2 = self._norm_aps(tt)

        def bcr(E, ps_sq=ps_sq):
            return E.matmul(ps_sq[:, :], b.ones_rb[0:1, :], b.rsr_bf[0:1, :],
                            start=True, stop=True)
        b.step("tensor", [rsr_w], bcr, inc=sPE)
        b.step("scalar", [(sPE, sPE.n)] + list(self.br_readers),
               lambda E, ps_sq=ps_sq: E.activation(
                   b.br_sb[:], ps_sq[:], AF.Copy), inc=sACT)
        self.bev_act = max(sACT.n, self._bmev)

        def mul_h(E, jh=0):
            sl = slice(jh * H2, (jh + 1) * H2)
            return E.tensor_tensor(y3[:, sl], sq3[:, sl], br3h, ALU.mult)
        b.step("vector", [(sACT, sACT.n)], _ft.partial(mul_h, jh=0),
               inc=sDVE)
        lo_ct = sDVE.n
        b.step("vector", [], _ft.partial(mul_h, jh=1), inc=sDVE)
        hi_ct = sDVE.n
        self.norm_dve = hi_ct
        self.br_readers = [(sDVE, hi_ct)]
        self.tile_act[tt] = [(sDVE, hi_ct)]
        self.tile_act_lo[tt] = (sDVE, lo_ct)
        if add_to_x:
            b.step("gpsimd", [(sDVE, hi_ct)],
                   lambda E: E.tensor_tensor(x3t, x3t, y3, ALU.add),
                   inc=sPOOL)
            self.xadd_pool = sPOOL.n
            self.tile_xadd[tt] = sPOOL.n

    def stage2b(self, tt, add_to_x, act_evac=False):
        self.stage2b_a(tt)
        self.stage2b_b(tt, add_to_x)


def ln_stage1(b, tt, cast_waits, sq_eng="gpsimd"):
    sACT, sDVE = b.sACT, b.sDVE
    ts0 = tt * TTILE
    xbf, sq_sb = b.xbf_sb, b.sq_sb
    xbf3 = xbf[:].rearrange("p (j t) -> p j t", j=DC)

    def cast(E, ts0=ts0, xbf3=xbf3):
        return E.activation(xbf3, _x3(b)[:, :, ts0:ts0 + TTILE], AF.Copy)
    b.step("scalar", list(cast_waits), cast, inc=sACT)
    cast_ct = sACT.n

    def sq(E):
        return E.tensor_tensor(sq_sb[:], xbf[:], xbf[:], ALU.mult)
    if sq_eng == "gpsimd":
        b.step("gpsimd", [(sACT, cast_ct)] + list(cast_waits), sq,
               inc=b.sPOOL)
        return [(sACT, cast_ct), (b.sPOOL, b.sPOOL.n)]
    b.step("vector", [(sACT, cast_ct)] + list(cast_waits), sq, inc=sDVE)
    return [(sACT, cast_ct), (sDVE, sDVE.n)]


def emit_ffn_weight_load(b, fb, g, buf, war_pe):
    sW = b.sW
    war = [(b.sPE, war_pe)]
    wg_t, wu_t, wd_t = fb["wg"][buf], fb["wu"][buf], fb["wd"][buf]

    def ld_g(E, g=g, wg_t=wg_t):
        return E.dma_start(
            out=wg_t[:].rearrange("p (j h) -> p j h", j=DC),
            in_=b.wgT_h.rearrange("(j p) h -> p j h", p=P)[:, :, g * HG:(g + 1) * HG])
    b.step("gpsimd", war, ld_g, inc=sW, amt=16)

    def ld_u(E, g=g, wu_t=wu_t):
        return E.dma_start(
            out=wu_t[:].rearrange("p (j h) -> p j h", j=DC),
            in_=b.wuT_h.rearrange("(j p) h -> p j h", p=P)[:, :, g * HG:(g + 1) * HG])
    b.step("gpsimd", [], ld_u, inc=sW, amt=16)

    def ld_d(E, g=g, wd_t=wd_t):
        return E.dma_start(
            out=wd_t[:].rearrange("p (q d) -> p q d", q=NHC),
            in_=b.wdT_h.rearrange("(q p) d -> p q d", p=P)[:, NHC * g:NHC * (g + 1), :])
    b.step("gpsimd", [], ld_d, inc=sW, amt=16)
    b.sw_group[g] = sW.n


def emit_ffn_phase(b, fb, gp, ln, tail_lnb, pre_s1_t0=None, post_s1=None,
                   attn_pre=None):
    sPE, sACT, sDVE, sW = b.sPE, b.sACT, b.sDVE, b.sW
    ps_gl, ps_ul, ps_dn = gp["g"], gp["u"], gp["dn"]
    h_sb, s_tmp = fb["h"], fb["stmp"]
    silu_hist = []
    hm_hist = []
    dn_hist = []
    it = 0
    b.dn_evac_tt = {}

    def block(g, tt, mid=(), pre=()):
        nonlocal it
        for fn in pre:
            fn()
        buf = g % 2
        wg_t, wu_t, wd_t = fb["wg"][buf], fb["wu"][buf], fb["wd"][buf]
        ts0 = tt * TTILE
        for hc in range(NHC):
            ps_g = ps_gl[it % 2]
            ps_u = ps_ul[it % 2]

            def gate(E, hc=hc, ts0=ts0, wg_t=wg_t, ps_g=ps_g, j0=0, j1=DC):
                last = None
                for j in range(j0, j1):
                    last = E.matmul(
                        ps_g[:, :],
                        wg_t[:, j * HG + hc * P: j * HG + hc * P + P],
                        b.y_sb[:, j * YROW + 1 + ts0:
                               j * YROW + 1 + ts0 + TTILE],
                        start=(j == 0), stop=(j == DC - 1))
                return last
            w = [(sW, b.sw_group[g]),
                 ln.tile_act_lo.get(tt, (b.sDVE, 0))]
            if len(silu_hist) >= 2:
                w.append((sACT, silu_hist[-2]))
            b.step("tensor", w, _ft.partial(gate, j0=0, j1=DC // 2))
            w2 = list(ln.tile_act[tt])
            b.step("tensor", w2, _ft.partial(gate, j0=DC // 2, j1=DC),
                   inc=sPE)
            gate_ct = sPE.n

            def up(E, hc=hc, ts0=ts0, wu_t=wu_t, ps_u=ps_u):
                last = None
                for j in range(DC):
                    last = E.matmul(
                        ps_u[:, :],
                        wu_t[:, j * HG + hc * P: j * HG + hc * P + P],
                        b.y_sb[:, j * YROW + 1 + ts0:
                               j * YROW + 1 + ts0 + TTILE],
                        start=(j == 0), stop=(j == DC - 1))
                return last
            w = []
            if len(hm_hist) >= 2:
                w.append((sDVE, hm_hist[-2]))
            b.step("tensor", w, up, inc=sPE)
            gu_ct = sPE.n

            st = s_tmp[it % 2]
            w = [(sPE, gate_ct)]
            if len(hm_hist) >= 2:
                w.append((sDVE, hm_hist[-2]))
            b.step("scalar", w,
                   lambda E, st=st, ps_g=ps_g: E.activation(
                       st[:], ps_g[:], AF.Silu, scale=SILU_SCALE), inc=sACT)
            silu_hist.append(sACT.n)

            hdst = h_sb[:, hc * T + ts0: hc * T + ts0 + TTILE]
            w = [(sACT, silu_hist[-1]), (sPE, gu_ct),
                 (sPE, b.pe_down_end.get((g - 1, tt), 0))]
            b.step("vector", w,
                   lambda E, hdst=hdst, st=st, ps_u=ps_u: E.tensor_tensor(
                       hdst, st[:], ps_u[:], ALU.mult), inc=sDVE)
            hm_hist.append(sDVE.n)
            it += 1
        b.pe_up_end[(g, tt)] = b.sPE.n
        for fn in mid:
            fn()
        hm4 = hm_hist[-NHC:]
        for m in range(DC):
            pd = ps_dn[m % 2]
            if m == 0:
                for q in range(NHC):
                    def dnq(E, q=q, ts0=ts0, wd_t=wd_t, pd=pd):
                        return E.matmul(
                            pd[:, :],
                            wd_t[:, q * D + 0 * P: q * D + 0 * P + P],
                            h_sb[:, q * T + ts0: q * T + ts0 + TTILE],
                            start=(q == 0), stop=(q == NHC - 1),
                            skip_group_check=True)
                    w = [(sDVE, hm4[q])]
                    if q == 0 and len(dn_hist) >= 2:
                        w.append((sDVE, dn_hist[-2]))
                    b.step("tensor", w, dnq,
                           inc=(sPE if q == NHC - 1 else None))
                dn_ct = sPE.n
            else:
                def dn(E, m=m, ts0=ts0, wd_t=wd_t, pd=pd):
                    last = None
                    for q in range(NHC):
                        last = E.matmul(
                            pd[:, :],
                            wd_t[:, q * D + m * P: q * D + m * P + P],
                            h_sb[:, q * T + ts0: q * T + ts0 + TTILE],
                            start=(q == 0), stop=(q == NHC - 1))
                    return last
                w = [(sDVE, hm_hist[-1])]
                if len(dn_hist) >= 2:
                    w.append((sDVE, dn_hist[-2]))
                b.step("tensor", w, dn, inc=sPE)
                dn_ct = sPE.n

            xs = b.x_sb[:, m * T + ts0: m * T + ts0 + TTILE]
            b.step("vector", [(sPE, dn_ct)],
                   lambda E, xs=xs, pd=pd: E.scalar_tensor_tensor(
                       xs, pd[:], DOWN_SCALE, xs, ALU.mult, ALU.add),
                   inc=sDVE)
            dn_hist.append(sDVE.n)
        b.pe_down_end[(g, tt)] = sPE.n
        b.dn_evac_tt[tt] = sDVE.n

    if pre_s1_t0 is not None:
        ln.s1ct[0] = pre_s1_t0
    else:
        ln.stage1(0, cast_waits=[(b.sX, getattr(b, "x_half_ct", 0))])
    ln.bev_dve = max(ln.bev_dve, getattr(b, "_init_mprep_ct", 0))
    cst = getattr(b, "c_ln_state", None) if pre_s1_t0 is not None else None
    if cst is not None:
        ln.stats_pe = cst["stats_pe"]
        ln.rowsct[0] = cst["rowsct0"]
        b.c_ln_state = None
    else:
        ln.stage2a(0)
        ln.stage2a_recip(0)
    ln.stage2b(0, add_to_x=False)
    ln.stage1(1, cast_waits=[(b.sX, getattr(b, "x_full_ct", 0))])
    xw = [(b.sX, getattr(b, "x_full_ct", 0))]
    mids = {
        0: [lambda: ln.stage2a(1)],
        1: [lambda: ln.stage2b_b(1, False)],
        2: [lambda: ln.stage2a(2)],
        3: [lambda: ln.stage2b_b(2, False)],
        4: [lambda: ln.stage2a(3)],
        5: [lambda: ln.stage2b_b(3, False)],
    }
    pres = {
        1: [lambda: ln.stage2b_a(1)],
        3: [lambda: ln.stage2b_a(2)],
        5: [lambda: ln.stage2b_a(3)],
    }
    hooks = {
        0: [lambda: ln.stage2a_recip(1)],
        1: [lambda: ln.stage1(2, cast_waits=xw)],
        2: [lambda: ln.stage2a_recip(2)],
        3: [lambda: ln.stage1(3, cast_waits=xw)],
        4: [lambda: ln.stage2a_recip(3)],
    }
    if post_s1 is not None:
        def _fin():
            post_s1()
            ln.bev_dve = max(ln.bev_dve, b.sDVE.n)
        hooks.setdefault(1, []).append(_fin)
    lnb_acts = {}
    if tail_lnb:
        def s1b(tt):
            return lambda: ln.stage1(tt, cast_waits=[(sDVE, dn_hist[-1])])

        def s2ab(tt):
            return lambda: ln.stage2a(tt)

        def s2rc(tt):
            return lambda: ln.stage2a_recip(tt)

        def s2ba(tt):
            return lambda: ln.stage2b_a(tt)

        def s2bbb(tt):
            def f():
                ln.stage2b_b(tt, add_to_x=True)
                lnb_acts[tt] = ln.tile_act[tt]
            return f

        def s2bb(tt):
            def f():
                ln.stage2b(tt, add_to_x=True)
                lnb_acts[tt] = ln.tile_act[tt]
            return f
        hooks[17] = [s1b(0)]
        mids[18] = [s2ab(0)]
        hooks[18] = [s2rc(0)]
        pres[19] = [s2ba(0)]
        mids[19] = [s2bbb(0)]
        hooks[19] = [s1b(1)]
        mids[20] = [s2ab(1)]
        hooks[20] = [s2rc(1)]
        pres[21] = [s2ba(1)]
        mids[21] = [s2bbb(1)]
        hooks[21] = [s1b(2)]
        mids[22] = [s2ab(2)]
        hooks[22] = [s2rc(2)]
        pres[23] = [s2ba(2)]
        mids[23] = [s2bbb(2)]
        hooks[23] = [lambda: s1b_dve(3)]

        def s1b_dve(tt):
            ln.stage1(tt, cast_waits=[(sDVE, dn_hist[-1])], sq_eng="vector")

        def tail3a():
            ln.stage2a(3)
            bev_live[:] = [(b.sDVE, ln.rows_dve), (sACT, ln.bev_act)]

        def tail3b():
            ln.rowsct[3] = (max(ln.rowsct[3][0], b.sDVE.n), ln.rowsct[3][1])
            ln.stage2a_recip(3)
            ln.stage2b(3, add_to_x=True)
            lnb_acts[3] = ln.tile_act[3]
            bev_live[:] = [(b.sDVE, ln.norm_dve), (sACT, ln.bev_act)]
        if attn_pre is not None:
            ww_t, wr_t = attn_pre
            wwr = b.wwT_h.rearrange("(j p) d -> p j d", p=P)
            ww3 = ww_t[:].rearrange("p (j d) -> p j d", j=DC)

            def ld_ww0():
                def f(E, ww3=ww3, wwr=wwr):
                    return E.dma_start(out=ww3[:, 0:DC // 2, :],
                                       in_=wwr[:, 0:DC // 2, :])
                b.step("gpsimd", [(b.sPE, b.pe_up_end[(NG - 2, NT - 1)])], f,
                       inc=b.sW, amt=16)
                b.aw_cts = [b.sW.n]

            def ld_ww1():
                def f1(E, ww3=ww3, wwr=wwr):
                    return E.dma_start(out=ww3[:, DC // 2:DC, :],
                                       in_=wwr[:, DC // 2:DC, :])
                b.step("gpsimd", [(b.sPE, b.pe_up_end[(NG - 1, NT - 1)])], f1,
                       inc=b.sW, amt=16)
                b.aw_cts.append(b.sW.n)

            def ld_wr():
                def f2(E, wr_t=wr_t):
                    return E.dma_start(
                        out=wr_t[:].rearrange("p (j d) -> p j d", j=DC),
                        in_=b.wrT_h.rearrange("(j p) d -> p j d", p=P))
                b.step("gpsimd", [(b.sPE, b.pe_down_end[(NG - 1, NT - 1)])],
                       f2, inc=b.sW, amt=16)
                b.aw_cts.append(b.sW.n)
            hooks[22] = [ld_ww0] + hooks.get(22, [])
            mids[23] = [ld_ww1] + mids.get(23, [])
            hooks[23] = [ld_wr] + hooks.get(23, [])

    bi = 0
    for pair in range(NG // 2):
        ga, gb_ = 2 * pair, 2 * pair + 1
        for tt in range(NT):
            for g in (ga, gb_):
                block(g, tt, mid=mids.get(bi, ()), pre=pres.get(bi, ()))
                for fn in hooks.get(bi, ()):
                    fn()
                bi += 1
            if tt == NT - 1:
                if ga + 2 < NG:
                    emit_ffn_weight_load(b, fb, ga + 2, ga % 2,
                                         b.pe_down_end[(ga, NT - 1)])
                if gb_ + 2 < NG:
                    emit_ffn_weight_load(b, fb, gb_ + 2, gb_ % 2,
                                         b.pe_down_end[(gb_, NT - 1)])
    if not tail_lnb:
        return None
    bev_live = [(b.sDVE, ln.bev_dve),
                (sACT, ln.bev_act)]
    return {
        "acts": lnb_acts,
        "xadd": ln.tile_xadd,
        "bev": bev_live,
        "tail3": (tail3a, tail3b),
        "silu": silu_hist[-1],
        "hm": hm_hist[-1],
        "dnev": dn_hist[-1],
        "pe": b.sPE.n,
    }


def emit_attn_weight_loads(b, ab, war_up, war_dn):
    sW = b.sW
    ww_t, wr_t = ab["ww"], ab["wr"]
    wwr = b.wwT_h.rearrange("(j p) d -> p j d", p=P)
    ww3 = ww_t[:].rearrange("p (j d) -> p j d", j=DC)

    def ld_ww0(E, ww3=ww3, wwr=wwr):
        return E.dma_start(out=ww3[:, 0:DC // 2, :], in_=wwr[:, 0:DC // 2, :])
    b.step("gpsimd", [(b.sPE, war_up)], ld_ww0, inc=sW, amt=16, no_snap=True)
    ww0 = sW.n

    def ld_ww1(E, ww3=ww3, wwr=wwr):
        return E.dma_start(out=ww3[:, DC // 2:DC, :], in_=wwr[:, DC // 2:DC, :])
    b.step("gpsimd", [], ld_ww1, inc=sW, amt=16)
    ww1 = sW.n

    def ld_wr(E, wr_t=wr_t):
        return E.dma_start(out=wr_t[:].rearrange("p (j d) -> p j d", j=DC),
                           in_=b.wrT_h.rearrange("(j p) d -> p j d", p=P))
    b.step("gpsimd", [(b.sPE, war_dn)], ld_wr, inc=sW, amt=16)
    return (ww0, ww1, sW.n)


def emit_attn_main(b, ab, ap, aw_ct, ytile, aend, pre_cast_c=False,
                   c_pre=None, warm_cb=None):
    sPE, sACT, sDVE = b.sPE, b.sACT, b.sDVE
    v_all, ww_t, wr_t = ab["v"], ab["ww"], ab["wr"]
    reads_bf, smt = ab["reads"], ab["smt"]
    ps_va, ps_vb, ps_s, ps_reads = ap["va"], ap["vb"], ap["s"], ap["reads"]

    vevac_ct = {}
    prev_vevac = [0]
    sm_par = {0: 0, 1: 0}
    intra_par = {0: 0, 1: 0}
    sm_done = {}
    xupd_prev = 0
    pre_s1 = None

    aww0, aww1, awr = aw_ct

    def emit_v_pair(Q):
        for kk in (2 * Q, 2 * Q + 1):
            for half, ps_v in ((0, ps_va), (1, ps_vb)):
                def vmm(E, kk=kk, half=half, ps_v=ps_v, ww_t=ww_t, j0=0,
                        j1=DC):
                    last = None
                    for j in range(j0, j1):
                        last = E.matmul(
                            ps_v[:, :],
                            b.y_sb[:, j * YROW + 1 + kk * KT:
                                   j * YROW + 1 + kk * KT + P],
                            ww_t[:, j * D + half * 512: j * D + half * 512 + 512],
                            start=(j == 0), stop=(j == DC - 1))
                    return last
                w = [(b.sW, aww0), (sACT, prev_vevac[0]),
                     (sACT, aend["silu"])] + list(ytile[kk // 4])
                b.step("tensor", w, _ft.partial(vmm, j0=0, j1=DC // 2),
                       no_snap=True)
                b.step("tensor", [(b.sW, aww1)],
                       _ft.partial(vmm, j0=DC // 2, j1=DC), inc=sPE,
                       no_snap=True)
                v_ct = sPE.n
                if kk == NKT - 1 and half == 1 and c_pre is not None:
                    wg0_t, wu0_t, wd0_t = c_pre

                    def ldg(E, wg0_t=wg0_t):
                        return E.dma_start(
                            out=wg0_t[:].rearrange("p (j h) -> p j h", j=DC),
                            in_=b.wgT_h.rearrange("(j p) h -> p j h", p=P)
                            [:, :, 0:HG])
                    b.step("gpsimd", [(sPE, v_ct)], ldg, inc=b.sW, amt=16)

                    def ldu(E, wu0_t=wu0_t):
                        return E.dma_start(
                            out=wu0_t[:].rearrange("p (j h) -> p j h", j=DC),
                            in_=b.wuT_h.rearrange("(j p) h -> p j h", p=P)
                            [:, :, 0:HG])
                    b.step("gpsimd", [], ldu, inc=b.sW, amt=16)
                    b.c_pre_gu_ct = b.sW.n

                vdst = v_all[:, kk * D + half * 512: kk * D + half * 512 + 512]
                b.step("scalar", [(sPE, v_ct)],
                       lambda E, vdst=vdst, ps_v=ps_v: E.activation(
                           vdst, ps_v[:], AF.Copy), inc=sACT, no_snap=True)
                vevac_ct[(kk, half)] = sACT.n
            prev_vevac[0] = vevac_ct[(kk, 1)]

    def emit_s_sm(Q, kt):
        par = kt % 2
        pss = ps_s[par]

        def smm(E, Q=Q, kt=kt, pss=pss):
            last = None
            for j in range(DC):
                last = E.matmul(
                    pss[:, :],
                    b.y_sb[:, j * YROW + kt * KT: j * YROW + kt * KT + P],
                    b.y_sb[:, j * YROW + 1 + Q * QW:
                           j * YROW + 1 + Q * QW + QW],
                    start=(j == 0), stop=(j == DC - 1))
            return last
        w = [(sDVE, sm_par[par]), (sDVE, aend["hm"])] \
            + list(ytile[max(kt // 4, Q // 2)])
        b.step("tensor", w, smm, inc=sPE, no_snap=True)
        s_ct = sPE.n

        mt_sl = b.mt_sb[:, (2 * Q - kt + 1) * P: (2 * Q - kt + 1) * P + QW]
        sdst = smt[par]
        b.step("vector", [(sPE, s_ct), (sPE, intra_par[par])],
               lambda E, sdst=sdst, pss=pss, mt_sl=mt_sl: E.tensor_tensor(
                   sdst[:], pss[:], mt_sl, ALU.mult), inc=sDVE, no_snap=True)
        sm_par[par] = sDVE.n
        sm_done[(Q, kt)] = sDVE.n

    emit_v_pair(0)
    emit_s_sm(0, 0)
    emit_s_sm(0, 1)
    if warm_cb is not None:
        warm_cb[0]()
    for Q in range(NQ):
        nkt = 2 * Q + 2
        for kt in range(nkt):
            par = kt % 2
            if (Q, kt) not in sm_done:
                emit_s_sm(Q, kt)
            if kt + 2 < nkt and (Q, kt + 2) not in sm_done:
                emit_s_sm(Q, kt + 2)
            sdst = smt[par]

            def intra(E, kt=kt, nkt=nkt, v_all=v_all, ps_reads=ps_reads,
                      sdst=sdst):
                last = None
                for m in range(DC):
                    last = E.matmul(
                        ps_reads[:, m * QW: (m + 1) * QW],
                        v_all[:, kt * D + m * P: kt * D + m * P + P],
                        sdst[:],
                        start=(kt == 0 and m % 2 == 0), stop=(kt == nkt - 1),
                        skip_group_check=True)
                return last
            w = [(sDVE, sm_done[(Q, kt)]), (sACT, vevac_ct[(kt, 1)]),
                 (sDVE, aend["dnev"])] + list(aend["bev"])
            if kt == 0:
                w.append((sDVE, xupd_prev))
            b.step("tensor", w, intra, inc=sPE, no_snap=True)
            intra_par[par] = sPE.n

        b.step("scalar", [(sPE, intra_par[(nkt - 1) % 2]),
                          (sDVE, aend["dnev"])],
               lambda E, reads_bf=reads_bf, ps_reads=ps_reads: E.activation(
                   reads_bf[:], ps_reads[:], AF.Copy), inc=sACT, no_snap=True)
        revac = sACT.n
        if Q + 1 < NQ:
            emit_v_pair(Q + 1)

        def rproj(E, wr_t=wr_t, reads_bf=reads_bf, ps_reads=ps_reads):
            last = None
            for m in range(DC):
                for j2 in range(DC):
                    last = E.matmul(
                        ps_reads[:, m * QW: (m + 1) * QW],
                        wr_t[:, j2 * D + m * P: j2 * D + m * P + P],
                        reads_bf[:, j2 * QW: (j2 + 1) * QW],
                        start=(j2 == 0), stop=(j2 == DC - 1),
                        skip_group_check=True)
            return last
        b.step("tensor", [(sACT, revac), (b.sW, awr)], rproj, inc=sPE,
               no_snap=True)
        rp_ct = sPE.n
        if Q == NQ - 1 and c_pre is not None:
            def ldd(E, wd0_t=c_pre[2]):
                return E.dma_start(
                    out=wd0_t[:].rearrange("p (q d) -> p q d", q=NHC),
                    in_=b.wdT_h.rearrange("(q p) d -> p q d", p=P)
                    [:, 0:NHC, :])
            b.step("gpsimd", [(sPE, rp_ct)], ldd, inc=b.sW, amt=16)
            b.c_pre_d_ct = b.sW.n

        def xupd(E, Q=Q, ps_reads=ps_reads, m0=0, m1=DC):
            last = None
            for m in range(m0, m1):
                xs = b.x_sb[:, m * T + Q * QW: m * T + (Q + 1) * QW]
                last = E.scalar_tensor_tensor(
                    xs, ps_reads[:, m * QW: (m + 1) * QW], b.al_col[:, 0:1],
                    xs, ALU.mult, ALU.add)
            return last
        b.step("vector", [(sPE, rp_ct),
                          (b.sPOOL, aend["xadd"].get(Q // 2, 0))],
               _ft.partial(xupd, m0=0, m1=DC // 2), inc=sDVE, no_snap=True)
        if Q + 1 < NQ:
            emit_s_sm(Q + 1, 0)
        b.step("vector", [], _ft.partial(xupd, m0=DC // 2, m1=DC), inc=sDVE,
               no_snap=True)
        xupd_prev = sDVE.n
        if Q + 1 < NQ:
            emit_s_sm(Q + 1, 1)
        if Q == 0 and warm_cb is not None:
            warm_cb[1]()
        if Q == 1 and pre_cast_c:
            pre_s1 = ln_stage1(b, 0, [(sDVE, xupd_prev), (sPE, aend["pe"])])
        last_revac = revac

    c_state = None
    if pre_cast_c and not getattr(b, "no_c_tail", False):
        psum_row = ps_reads[0:1, 4 * QW:4 * QW + TTILE]
        pssq_row = ps_reads[0:1, 6 * QW:6 * QW + TTILE]

        def cstats(E, psum_row=psum_row, pssq_row=pssq_row):
            for j in range(DC):
                E.matmul(psum_row, b.ones_cb[:, 0:1],
                         b.xbf_sb[:, j * TTILE:(j + 1) * TTILE],
                         start=(j == 0), stop=(j == DC - 1))
            last = None
            for j in range(DC):
                last = E.matmul(pssq_row, b.ones_cb[:, 0:1],
                                b.sq_sb[:, j * TTILE:(j + 1) * TTILE],
                                start=(j == 0), stop=(j == DC - 1))
            return last
        b.step("tensor", list(pre_s1) + [(sDVE, xupd_prev),
                                         (sACT, last_revac)],
               cstats, inc=sPE, no_snap=True)
        c_stats_pe = sPE.n

        def crows(E, psum_row=psum_row, pssq_row=pssq_row):
            E.tensor_scalar(b.mrow[0:1, :], psum_row, 1.0 / D, None, ALU.mult)
            E.tensor_tensor(b.rrow[0:1, :], b.mrow[0:1, :], b.mrow[0:1, :],
                            ALU.mult)
            E.scalar_tensor_tensor(b.vrow[0:1, :], pssq_row, 1.0 / D,
                                   b.rrow[0:1, :], ALU.mult, ALU.subtract)
            return E.tensor_copy(b.mrow_bf[0:1, :], b.mrow[0:1, :])
        b.step("vector", [(sPE, c_stats_pe)], crows, inc=sDVE, no_snap=True)
        c_rows_dve = sDVE.n

        b.step("scalar", [(sDVE, c_rows_dve)],
               lambda E: E.activation(b.rrow[0:1, :], b.vrow[0:1, :], AF.Sqrt,
                                      bias=b.eps_col[0:1, 0:1], scale=1.0),
               inc=sACT, no_snap=True)
        c_sqrt = sACT.n

        def crecip(E):
            E.reciprocal(b.vrow[0:1, :], b.rrow[0:1, :])
            return E.tensor_copy(b.rsr_bf[0:1, :], b.vrow[0:1, :])
        b.step("vector", [(sACT, c_sqrt)], crecip, inc=sDVE, no_snap=True)
        c_state = {"stats_pe": c_stats_pe,
                   "rowsct0": (c_rows_dve, (sDVE, sDVE.n))}
    b.c_ln_state = c_state
    return pre_s1 if pre_cast_c else None


def emit_init(b, ps_init):
    sX, sPE, sACT, sDVE = b.sX, b.sPE, b.sACT, b.sDVE

    xr = b.xT_h.rearrange("(j p) t -> p j t", p=P)
    b.step("sync", [], lambda E: E.dma_start(out=b.scal_sb[0:1, 0:1], in_=b.dec_h),
           inc=sX, amt=16)
    b.step("sync", [], lambda E: E.dma_start(out=b.scal_sb[0:1, 1:2], in_=b.la_h),
           inc=sX, amt=16)
    scal_dma_ct = sX.n

    b.step("sync", [], lambda E: E.dma_start(out=b.g_sb[:], in_=b.G_h), inc=sX,
           amt=16)
    b.step("sync", [], lambda E: E.dma_start(out=b.caus_sb[:], in_=b.caus_h),
           inc=sX, amt=16)
    g_ct = sX.n

    def ld_x0a(E, xr=xr):
        return E.dma_start(out=_x3(b)[:, 0:4, 0:TTILE], in_=xr[:, 0:4, 0:TTILE])
    b.step("sync", [], ld_x0a, inc=sX, amt=16)

    def ld_x0b(E, xr=xr):
        return E.dma_start(out=_x3(b)[:, 4:8, 0:TTILE], in_=xr[:, 4:8, 0:TTILE])
    b.step("scalar", [], ld_x0b, inc=sX, amt=16)
    b.x_half_ct = sX.n
    x_ct = sX.n

    def ld_x1(E, xr=xr):
        return E.dma_start(out=_x3(b)[:, :, TTILE:T], in_=xr[:, :, TTILE:T])
    b.step("sync", [], ld_x1, inc=sX, amt=16)
    b.x_full_ct = sX.n

    def consts(E):
        E.memset(b.ones_f[:], 1.0)
        E.memset(b.ones_rb[:], 1.0)
        E.memset(b.ones_cb[:], 1.0)
        E.memset(b.eps_col[:], EPS)
        E.memset(b.mt_sb[:, 0:P], 0.0)
        last = None
        for j in range(DC):
            last = E.memset(b.y_sb[:, j * YROW: j * YROW + 1], 0.0)
        return last
    b.step("vector", [], consts, inc=sDVE)
    const_ct = sDVE.n

    def scal(E):
        E.activation(b.scal_sb[0:1, 2:3], b.scal_sb[0:1, 0:1], AF.Sigmoid)
        E.activation(b.scal_sb[0:1, 3:4], b.scal_sb[0:1, 2:3], AF.Ln)
        return E.activation(b.scal_sb[0:1, 4:5], b.scal_sb[0:1, 1:2], AF.Exp)
    b.step("scalar", [(sX, scal_dma_ct)], scal, inc=sACT)
    scal_ct = sACT.n

    def bc(E, ps_init=ps_init):
        E.matmul(ps_init[:, 0:1], b.ones_f[0:1, :], b.scal_sb[0:1, 3:4],
                 start=True, stop=True)
        return E.matmul(ps_init[:, 1:2], b.ones_f[0:1, :], b.scal_sb[0:1, 4:5],
                        start=False, stop=True)
    b.step("tensor", [(sACT, scal_ct), (sDVE, const_ct)], bc, inc=sPE)
    bc_ct = sPE.n

    def mprep_dve(E, ps_init=ps_init):
        E.tensor_copy(b.lg_col[:], ps_init[:, 0:1])
        E.tensor_copy(b.al_col[:], ps_init[:, 1:2])
        for i in range(1, 17):
            E.tensor_scalar(b.bias_cols[:, i:i + 1], b.lg_col[:],
                            float(128 * (i - 1) - 1), None, ALU.mult)
        E.tensor_scalar(b.ytmp[0][:, 0:P], b.g_sb[:], b.lg_col[:, 0:1],
                        b.lg_col[:, 0:1], ALU.mult, ALU.subtract)
        return E.tensor_scalar(b.ytmp[1][:, 0:P], b.ytmp[0][:, 0:P], 0.0, None,
                               ALU.min)
    b.step("vector", [(sPE, bc_ct), (sX, g_ct)], mprep_dve, inc=sDVE)
    b._init_mprep_ct = sDVE.n


def emit_init_mprep(b, ps_init):
    sPE, sACT, sDVE = b.sPE, b.sACT, b.sDVE
    mprep_ct = b._init_mprep_ct

    def mprep_act(E):
        E.activation(b.ytmp[0][:, 0:P], b.ytmp[1][:, 0:P], AF.Exp)
        last = None
        for i in range(2, 17):
            last = E.activation(b.mt_sb[:, i * P:(i + 1) * P], b.g_sb[:], AF.Exp,
                                bias=b.bias_cols[:, i:i + 1], scale=b.lg_col[:, 0:1])
        return last
    b.step("scalar", [(sDVE, mprep_ct)], mprep_act, inc=sACT)
    mact_ct = sACT.n

    b.step("vector", [(sACT, mact_ct)],
           lambda E: E.tensor_tensor(b.mt_sb[:, P:2 * P], b.ytmp[0][:, 0:P],
                                     b.caus_sb[:], ALU.mult), inc=sDVE)


def build(phases=None):
    phases = PHASES if phases is None else phases
    nc = bass.Bass("TRN2", target_bir_lowering=False, debug=False)
    b = _B()

    b.xT_h = nc.dram_tensor("xT", [D, T], F32, kind="ExternalInput").ap()
    b.wwT_h = nc.dram_tensor("wwT", [D, D], F32, kind="ExternalInput").ap()
    b.wrT_h = nc.dram_tensor("wrT", [D, D], F32, kind="ExternalInput").ap()
    b.wgT_h = nc.dram_tensor("wgT", [D, H], I32, kind="ExternalInput").ap()
    b.wuT_h = nc.dram_tensor("wuT", [D, H], I32, kind="ExternalInput").ap()
    b.wdT_h = nc.dram_tensor("wdT", [H, D], I32, kind="ExternalInput").ap()
    b.dec_h = nc.dram_tensor("dec", [1, 1], F32, kind="ExternalInput").ap()
    b.la_h = nc.dram_tensor("la", [1, 1], F32, kind="ExternalInput").ap()
    b.G_h = nc.dram_tensor("G", [P, P], F32, kind="ExternalInput").ap()
    b.caus_h = nc.dram_tensor("causT", [P, P], F32, kind="ExternalInput").ap()
    b.yT_h = nc.dram_tensor("yT", [D, T], F32, kind="ExternalOutput").ap()

    with ExitStack() as ctx:
        e = ctx.enter_context
        b.x_sb = e(nc.sbuf_tensor("x_sb", [P, DC * T], F32))
        b.y_sb = e(nc.sbuf_tensor("y_sb", [P, DC * YROW], BF16))
        b.xbf_sb = e(nc.sbuf_tensor("xbf_sb", [P, DC * TTILE], BF16))
        b.sq_sb = e(nc.sbuf_tensor("sq_sb", [P, DC * TTILE], BF16))
        b.mt_sb = e(nc.sbuf_tensor("mt_sb", [P, 17 * P], F32))
        b.bm_sb = e(nc.sbuf_tensor("bm_sb", [P, TTILE], BF16))
        b.br_sb = e(nc.sbuf_tensor("br_sb", [P, TTILE], BF16))
        b.mrow_bf = e(nc.sbuf_tensor("mrow_bf", [1, TTILE], BF16))
        b.rsr_bf = e(nc.sbuf_tensor("rsr_bf", [1, TTILE], BF16))
        b.ones_rb = e(nc.sbuf_tensor("ones_rb", [1, P], BF16))
        b.ytmp = [e(nc.sbuf_tensor(f"ytmp{i}", [P, TTILE], F32)) for i in range(2)]
        b.g_sb = e(nc.sbuf_tensor("g_sb", [P, P], F32))
        b.caus_sb = e(nc.sbuf_tensor("caus_sb", [P, P], F32))
        b.ones_f = e(nc.sbuf_tensor("ones_f", [1, P], F32))
        b.ones_cb = e(nc.sbuf_tensor("ones_cb", [P, 1], BF16))
        b.lg_col = e(nc.sbuf_tensor("lg_col", [P, 1], F32))
        b.al_col = e(nc.sbuf_tensor("al_col", [P, 1], F32))
        b.eps_col = e(nc.sbuf_tensor("eps_col", [P, 1], F32))
        b.bias_cols = e(nc.sbuf_tensor("bias_cols", [P, 17], F32))
        b.scal_sb = e(nc.sbuf_tensor("scal_sb", [1, 8], F32))
        b.mrow = e(nc.sbuf_tensor("mrow", [1, TTILE], F32))
        b.vrow = e(nc.sbuf_tensor("vrow", [1, TTILE], F32))
        b.rrow = e(nc.sbuf_tensor("rrow", [1, TTILE], F32))

        b.sX = _Sem(e(nc.semaphore("sX")))
        b.sW = _Sem(e(nc.semaphore("sW")))
        b.sO = _Sem(e(nc.semaphore("sO")))
        b.sPE = _Sem(e(nc.semaphore("sPE")))
        b.sACT = _Sem(e(nc.semaphore("sACT")))
        b.sDVE = _Sem(e(nc.semaphore("sDVE")))
        b.sPOOL = _Sem(e(nc.semaphore("sPOOL")))
        b.all_sems = [b.sX, b.sW, b.sO, b.sPE, b.sACT, b.sDVE, b.sPOOL]
        b.sw_group = {}
        b.pe_down_end = {}
        b.pe_up_end = {}

        b.begin_phase()
        with nc.psum_tensor("ps_init", [P, 4], F32) as ps_init:
            emit_init(b, ps_init)
        xbf3 = b.xbf_sb[:].rearrange("p (j t) -> p j t", j=DC)

        def cast0(E, xbf3=xbf3):
            return E.activation(xbf3, _x3(b)[:, :, 0:TTILE], AF.Copy)
        b.step("scalar", [(b.sX, b.x_half_ct)], cast0, inc=b.sACT)
        _xbf_act = b.sACT.n

        def sq0(E):
            return E.tensor_tensor(b.sq_sb[:], b.xbf_sb[:], b.xbf_sb[:],
                                   ALU.mult)
        b.step("vector", [(b.sACT, _xbf_act)], sq0, inc=b.sDVE)
        pre_s1_first = [(b.sACT, _xbf_act), (b.sDVE, b.sDVE.n)]
        init_state = {"ps_init": ps_init, "done": False}

        def finish_init():
            if not init_state["done"]:
                emit_init_mprep(b, init_state["ps_init"])
                init_state["done"] = True

        first_phase = True
        lnb_carry = None
        pre_s1_carry = pre_s1_first
        pi = 0
        for idx, ph in enumerate(phases):
            pi += 1
            nxt = phases[idx + 1] if idx + 1 < len(phases) else None
            if ph == "ffn":
                cp = getattr(b, "c_pre", None) if not first_phase else None
                with ExitStack() as fs:
                    f = fs.enter_context
                    if cp is not None:
                        fb = {
                            "h": f(nc.sbuf_tensor(f"h_sb{pi}", [P, NHC * T],
                                                  BF16)),
                            "wg": [cp[0],
                                   f(nc.sbuf_tensor(f"wg{pi}_1", [P, DC * HG],
                                                    BF16))],
                            "wu": [cp[1],
                                   f(nc.sbuf_tensor(f"wu{pi}_1", [P, DC * HG],
                                                    BF16))],
                        }
                        f(nc.sbuf_tensor(f"hold{pi}", [P, 3 * 4096], BF16))
                        fb["wd"] = [cp[2],
                                    f(nc.sbuf_tensor(f"wd{pi}_1", [P, NHC * D],
                                                     BF16))]
                        fb["stmp"] = [f(nc.sbuf_tensor(f"st{pi}_{i}",
                                                       [P, TTILE], BF16))
                                      for i in range(2)]
                        b.c_pre = None
                    else:
                        fb = {
                            "h": f(nc.sbuf_tensor(f"h_sb{pi}", [P, NHC * T],
                                                  BF16)),
                            "wg": [f(nc.sbuf_tensor(f"wg{pi}_{i}",
                                                    [P, DC * HG], BF16))
                                   for i in range(2)],
                            "wu": [f(nc.sbuf_tensor(f"wu{pi}_{i}",
                                                    [P, DC * HG], BF16))
                                   for i in range(2)],
                            "wd": [f(nc.sbuf_tensor(f"wd{pi}_{i}",
                                                    [P, NHC * D], BF16))
                                   for i in range(2)],
                            "stmp": [f(nc.sbuf_tensor(f"st{pi}_{i}",
                                                      [P, TTILE], BF16))
                                     for i in range(2)],
                        }
                    if not first_phase:
                        b.begin_phase()
                    b.pe_down_end = {}
                    b.pe_up_end = {}
                    b.sw_group = {}
                    if cp is not None:
                        b.sw_group[0] = max(b.c_pre_gu_ct, b.c_pre_d_ct)
                    else:
                        emit_ffn_weight_load(b, fb, 0, 0, 0)
                    emit_ffn_weight_load(b, fb, 1, 1, 0)
                    with ExitStack() as ls:
                        l = ls.enter_context
                        gp = {
                            "g": [l(nc.psum_tensor(f"ps_g{pi}_{i}",
                                                   [P, TTILE], F32))
                                  for i in range(2)],
                            "u": [l(nc.psum_tensor(f"ps_u{pi}_{i}",
                                                   [P, TTILE], F32))
                                  for i in range(2)],
                            "dn": [l(nc.psum_tensor(f"ps_dn{pi}_{i}",
                                                    [P, TTILE], F32))
                                   for i in range(2)],
                        }
                        ps_sum = l(nc.psum_tensor(f"ps_sum{pi}", [P, TTILE], F32))
                        ps_sq = l(nc.psum_tensor(f"ps_sq{pi}", [P, TTILE], F32))
                        ln = _LN(b, ps_sum, ps_sq)
                        attn_pre = None
                        if nxt == "attn":
                            base = nc.lookup_mloc(fb["h"]).addr
                            ww_pre = nc.alloc_sbuf_tensor_at(
                                "ww_pre", [P, DC * D], BF16,
                                offset=base + 4 * 8192)
                            wr_pre = nc.alloc_sbuf_tensor_at(
                                "wr_pre", [P, DC * D], BF16,
                                offset=base + 6 * 8192)
                            attn_pre = (ww_pre, wr_pre)
                            b.attn_pre = attn_pre
                        lnb_carry = emit_ffn_phase(
                            b, fb, gp, ln, nxt == "attn",
                            pre_s1_t0=pre_s1_carry,
                            post_s1=(finish_init if not init_state["done"]
                                     else None), attn_pre=attn_pre)
                        pre_s1_carry = None
            elif ph == "attn":
                with ExitStack() as as_:
                    a = as_.enter_context
                    pre = getattr(b, "attn_pre", None) if lnb_carry is not None \
                        else None
                    ab = {"v": a(nc.sbuf_tensor("v_all", [P, NKT * D], BF16))}
                    if pre is not None:
                        a(nc.sbuf_tensor("wwr_hold", [P, DC * D * 2], BF16))
                        ab["ww"], ab["wr"] = pre
                    else:
                        ab["ww"] = a(nc.sbuf_tensor("ww_bf", [P, DC * D], BF16))
                        ab["wr"] = a(nc.sbuf_tensor("wr_bf", [P, DC * D], BF16))
                    ab["reads"] = a(nc.sbuf_tensor("reads_bf", [P, DC * QW],
                                                   BF16))
                    ab["smt"] = [a(nc.sbuf_tensor(f"smt{i}", [P, QW], BF16))
                                 for i in range(2)]
                    finish_init()
                    if pre is not None:
                        aw_ct = tuple(b.aw_cts)
                    else:
                        war_up = b.pe_up_end.get((NG - 1, NT - 1), 0)
                        war_dn = b.pe_down_end.get((NG - 1, NT - 1), 0)
                        aw_ct = emit_attn_weight_loads(b, ab, war_up, war_dn)
                    if lnb_carry is not None:
                        ytile = lnb_carry["acts"]
                        aend = {k: lnb_carry[k] for k in
                                ("xadd", "bev", "silu", "hm", "dnev", "pe")}
                    else:
                        with ExitStack() as ls:
                            l = ls.enter_context
                            ps_sum = l(nc.psum_tensor("ps_sumB", [P, TTILE], F32))
                            ps_sq = l(nc.psum_tensor("ps_sqB", [P, TTILE], F32))
                            b.begin_phase()
                            lnx = _LN(b, ps_sum, ps_sq)
                            for tt in range(NT):
                                lnx.tile(tt, add_to_x=True)
                            ytile = dict(lnx.tile_act)
                            aend = {"xadd": dict(lnx.tile_xadd),
                                    "bev": [(b.sDVE, lnx.bev_dve),
                                            (b.sACT, lnx.bev_act)],
                                    "silu": b.sACT.n,
                                    "hm": b.sDVE.n, "dnev": b.sDVE.n,
                                    "pe": b.sPE.n}
                    with ExitStack() as ms:
                        m_ = ms.enter_context
                        ap = {
                            "va": m_(nc.psum_tensor("ps_va", [P, 512], F32)),
                            "vb": m_(nc.psum_tensor("ps_vb", [P, 512], F32)),
                            "s": [m_(nc.psum_tensor(f"ps_s{i}", [P, QW], F32))
                                  for i in range(2)],
                            "reads": m_(nc.psum_tensor("ps_reads", [P, DC * QW],
                                                       F32)),
                        }
                        c_pre = None
                        if nxt == "ffn":
                            base_b = nc.lookup_mloc(ab["v"]).addr
                            c_pre = (
                                nc.alloc_sbuf_tensor_at(
                                    "c_wg0", [P, DC * HG], BF16,
                                    offset=base_b + 4 * 8192),
                                nc.alloc_sbuf_tensor_at(
                                    "c_wu0", [P, DC * HG], BF16,
                                    offset=base_b + 5 * 8192),
                                nc.alloc_sbuf_tensor_at(
                                    "c_wd0", [P, NHC * D], BF16,
                                    offset=base_b + 6 * 8192),
                            )
                            b.c_pre = c_pre
                        pre_s1_carry = emit_attn_main(
                            b, ab, ap, aw_ct, ytile, aend,
                            pre_cast_c=(nxt == "ffn"), c_pre=c_pre,
                            warm_cb=(lnb_carry or {}).get("tail3"))
                lnb_carry = None
            else:
                raise ValueError(ph)
            first_phase = False

        yr = b.yT_h.rearrange("(j p) t -> p j t", p=P)
        if phases[-1] == "ffn" and len(getattr(b, "dn_evac_tt", {})) == NT:
            for tt in range(NT):
                ts0 = tt * TTILE

                def out_t(E, ts0=ts0):
                    return E.dma_start(out=yr[:, :, ts0:ts0 + TTILE],
                                       in_=_x3(b)[:, :, ts0:ts0 + TTILE])
                b.step("sync", [(b.sDVE, b.dn_evac_tt[tt])], out_t,
                       inc=b.sO, amt=16)
            b.step("sync", [(b.sO, b.sO.n)], lambda E: E.nop())
        else:
            b.begin_phase()

            def out(E):
                return E.dma_start(out=yr, in_=_x3(b))
            b.step("sync", [], out, inc=b.sO, amt=16)
            b.step("sync", [(b.sO, b.sO.n)], lambda E: E.nop())

        block = e(nc.Block())
        for name in _ENGINES:
            fns = b.prog[name]
            getattr(block, name)(lambda E, fns=fns: [f(E) for f in fns])
    return nc


def kernel(L0_attn_out, L0_attn_W_write, L0_attn_W_read, L0_attn_decay,
           L0_attn_log_alpha, L0_ffn_w_gate, L0_ffn_w_up, L0_ffn_w_down):
    x = np.ascontiguousarray(np.asarray(L0_attn_out, dtype=np.float32))
    wwT = np.ascontiguousarray(np.asarray(L0_attn_W_write, dtype=np.float32).T)
    wrT = np.ascontiguousarray(np.asarray(L0_attn_W_read, dtype=np.float32).T)
    wgT = np.ascontiguousarray(np.asarray(L0_ffn_w_gate, dtype=np.int32).T)
    wuT = np.ascontiguousarray(np.asarray(L0_ffn_w_up, dtype=np.int32).T)
    wdT = np.ascontiguousarray(np.asarray(L0_ffn_w_down, dtype=np.int32).T)
    dec = np.asarray(L0_attn_decay, dtype=np.float32).reshape(1, 1)
    la = np.asarray(L0_attn_log_alpha, dtype=np.float32).reshape(1, 1)
    idx = np.arange(P, dtype=np.float32)
    G = (idx[None, :] - idx[:, None]).astype(np.float32)
    caus = (idx[None, :] > idx[:, None]).astype(np.float32)

    nc = build()
    in_maps = []
    for bb in range(N_CORES):
        in_maps.append({
            "xT": np.ascontiguousarray(x[bb].T), "wwT": wwT, "wrT": wrT,
            "wgT": wgT, "wuT": wuT, "wdT": wdT, "dec": dec, "la": la,
            "G": G, "causT": caus,
        })
    res = run_bass_kernel_spmd(nc, in_maps, core_ids=list(range(N_CORES)))
    out = np.stack([np.asarray(res.results[bb]["yT"]).T for bb in range(N_CORES)],
                   axis=0)
    return np.ascontiguousarray(out, dtype=np.float32)
